# version 17
# baseline (speedup 1.0000x reference)
import contextlib

import numpy as np
import concourse.bass as bass
import concourse.mybir as mybir
from concourse.bass_utils import run_bass_kernel_spmd

F32 = mybir.dt.float32
BF16 = mybir.dt.bfloat16
I32 = mybir.dt.int32
AF = mybir.ActivationFunctionType
ALU = mybir.AluOpType
AX = mybir.AxisListType

D = 2048
NCH = 16
SEQ = 4096
TOWN = 2048
QT = 512
NQT = 4
KBMAX = (8, 16, 24, 32)
NH = 16
DH = 128
GH = 4
NG = 4
SCALE = float(1.0 / np.sqrt(128.0))
EPS_LN = 1e-5
EPS_R = 1e-8
FT = 1024
NFT = 2
OWN_BLOCKS = ((0, 3, 4, 7), (1, 2, 5, 6))
ENGS = ("sync", "tensor", "scalar", "vector", "gpsimd")


def _mm2(e, out_ap, lhsT, rhs, start, stop):
    n = rhs.shape[-1]
    ins = None
    for off in range(0, n, 512):
        w = min(512, n - off)
        ins = e.matmul(out_ap[:, off:off + w], lhsT, rhs[:, off:off + w],
                       start=start, stop=stop)
    return ins


ALL_PHASES = ("W0", "A0b", "A1q", "A0", "A1k", "A1v", "A2", "A3", "B")


def build_program(phases=ALL_PHASES, taps=(), ffn_taps=False):
    nc = bass.Bass("TRN2", target_bir_lowering=False, debug=False)

    def din(name, shape):
        return nc.dram_tensor(name, shape, F32, kind="ExternalInput").ap()

    xT = din("xT", [D, SEQ])
    xTown = din("xTown", [D, TOWN])
    xQown = din("xQown", [D, TOWN])
    negdelta = din("negdelta", [1, 32])
    Wq = din("Wq", [D, D]); Wk = din("Wk", [D, D]); Wv = din("Wv", [D, D])
    Wo = din("Wo", [D, D])
    bq = din("bq", [D]); bk = din("bk", [D]); bv = din("bv", [D]); bo = din("bo", [D])
    w_router = din("w_router", [D, 128])
    w_gate = din("w_gate", [D, 8192]); w_up = din("w_up", [D, 8192])
    w_down = din("w_down", [1024, D])
    amp2 = din("amp2", [128, 16]); esc = din("esc", [1, 128])
    swg = din("swg", [D, D]); swu = din("swu", [D, D]); swd = din("swd", [D, D])
    yT = nc.dram_tensor("yT", [D, TOWN], F32, kind="ExternalOutput").ap()

    q_scr = nc.dram_tensor("q_scr", [128, NH * TOWN], BF16, kind="Internal").ap()
    k_scr = nc.dram_tensor("k_scr", [128, NH * SEQ], BF16, kind="Internal").ap()
    v_scr = nc.dram_tensor("v_scr", [128, 32 * D], BF16, kind="Internal").ap()
    x2_scr = nc.dram_tensor("x2_scr", [128, NCH * TOWN], F32, kind="Internal").ap()
    nrm_scr = nc.dram_tensor("nrm_scr", [2, QT], F32, kind="Internal").ap()
    ln_scr = nc.dram_tensor("ln_scr", [2, 2 * QT], F32, kind="Internal").ap()

    tap_outs = {}
    if ffn_taps:
        for t, shape, dt in (("xln2", [128, NCH * FT], BF16),
                             ("mT", [128, 8 * 128], BF16),
                             ("abc", [128, 8 * FT], BF16),
                             ("wacc", [128, 8 * FT], F32),
                             ("smid", [128, NCH * FT], BF16)):
            tap_outs[t] = nc.dram_tensor("tap_" + t, shape, dt,
                                         kind="ExternalOutput").ap()
    for t in taps:
        shape, dt = {
            "xlnO": ([128, NCH * TOWN], BF16), "xlnP": ([128, NCH * SEQ], BF16),
            "q_scr": ([128, NH * TOWN], BF16), "k_scr": ([128, NH * SEQ], BF16),
            "v_scr": ([128, 32 * D], BF16), "attnout": ([128, NH * TOWN], BF16),
            "x2": ([128, NCH * TOWN], F32)}[t]
        tap_outs[t] = nc.dram_tensor("tap_" + t, shape, dt, kind="ExternalOutput").ap()


    _uid_ctr = [0]
    def _u(name):
        _uid_ctr[0] += 1
        return f"{name}_{_uid_ctr[0]}"

    class Ct:
        __slots__ = ("h", "n", "eng")
        def __init__(self, name, eng):
            self.h = nc.alloc_semaphore(name=name)
            self.n = 0
            self.eng = eng

    dS = Ct("dS", None)
    dG = Ct("dG", None)
    dA = Ct("dA", None)
    cts = {"tensor": Ct("sPE", "tensor"), "scalar": Ct("sSC", "scalar"),
           "vector": Ct("sDV", "vector"), "gpsimd": Ct("sPO", "gpsimd"),
           "sync": Ct("sSY", "sync")}

    ready = {}
    free = {}

    class Sched:
        def __init__(self):
            self.ev = {e: [] for e in ENGS}

        def add(self, eng, fn, ins=(), outs=(), extra=(), dma=None):
            waits = {}

            def addw(w):
                prev = waits.get(id(w[0]))
                waits[id(w[0])] = (w[0], max(w[1], prev[1] if prev else 0))

            for k in ins:
                w = ready.get(k)
                if w:
                    addw(w)
            for k in outs:
                for w in free.get(k, ()):
                    addw(w)
                w = ready.get(k)
                if w:
                    addw(w)
            for w in extra:
                addw(w)
            if dma is not None:
                ct, delta = dma, 16
            else:
                ct, delta = cts[eng], 1
            ct.n += delta
            self.ev[eng].append((fn, [(c, v) for c, v in waits.values()], ct, delta))
            mark = (ct, ct.n)
            for k in outs:
                ready[k] = mark
                free[k] = []
            for k in ins:
                free.setdefault(k, []).append(mark)

        def flush(self):
            with nc.Block() as blk:
                for eng in ENGS:
                    evs = self.ev[eng]
                    if not evs:
                        continue

                    def body(e, evs=evs):
                        for fn, waits, ct, delta in evs:
                            for c, v in waits:
                                e.wait_ge(c.h, v)
                            ins = fn(e)
                            if ins is None:
                                ins = e.nop()
                            ins.then_inc(ct.h, delta)

                    getattr(blk, eng)(body)
            self.ev = {e: [] for e in ENGS}

    ones_bf = nc.alloc_sbuf_tensor("c_ones_bf", [128, 1], BF16)
    ones_row = nc.alloc_sbuf_tensor("c_ones_row", [1, 128], F32)
    one128 = nc.alloc_sbuf_tensor("c_one128", [128, 128], BF16)
    iota_sw = nc.alloc_sbuf_tensor("c_iota_sw", [128, QT], F32)
    nd_bc = nc.alloc_sbuf_tensor("c_nd_bc", [128, 32], F32)
    ident_bf = nc.alloc_sbuf_tensor("c_ident", [128, 128], BF16)
    bq_cols = nc.alloc_sbuf_tensor("c_bq", [128, 16], F32)
    bk_cols = nc.alloc_sbuf_tensor("c_bk", [128, 16], F32)
    bo_cols = nc.alloc_sbuf_tensor("c_bo", [128, 16], F32)
    esc_bc = nc.alloc_sbuf_tensor("c_esc_bc", [128, 128], F32)
    mixM = nc.alloc_sbuf_tensor("c_mixM", [128, 8 * 128], BF16)

    if "W0" in phases:
        with (
            nc.sbuf_tensor(_u("w0_ioti"), [128, QT], I32) as ioti,
            nc.sbuf_tensor(_u("w0_amp"), [128, 16], F32) as amp_sb,
            nc.sbuf_tensor(_u("w0_p8"), [128, 8], F32) as p8,
            nc.sbuf_tensor(_u("w0_r8"), [128, 1], F32) as r8,
            nc.sbuf_tensor(_u("w0_esc_row"), [1, 128], F32) as esc_row,
            nc.sbuf_tensor(_u("w0_nd_row"), [1, 32], F32) as nd_row,
            nc.psum_tensor(_u("w0_psa"), [128, 128], F32) as psa,
            nc.psum_tensor(_u("w0_psb"), [128, 32], F32) as psb,
        ):
            s = Sched()
            s.add("sync", lambda e: e.dma_start(amp_sb[:], amp2), outs=["amp"], dma=dS)
            s.add("sync", lambda e: e.dma_start(esc_row[:], esc), outs=["escr"], dma=dS)
            s.add("sync", lambda e: e.dma_start(nd_row[:], negdelta), outs=["ndr"], dma=dS)
            for c in range(16):
                for nm, col, src in (("bq", bq_cols, bq), ("bk", bk_cols, bk),
                                     ("bo", bo_cols, bo)):
                    s.add("sync", lambda e, col=col, src=src, c=c: e.dma_start(
                        col[:, c:c + 1],
                        src[c * 128:(c + 1) * 128].rearrange("(p o) -> p o", o=1)),
                        outs=[nm], dma=dS)

            def g_iota(e):
                e.iota(ioti[:], [[1, QT]], base=0, channel_multiplier=-1)
                e.drain()
                e.tensor_scalar(out=ident_bf[:], in0=ioti[:, 0:128],
                                scalar1=0, scalar2=None, op0=ALU.is_equal)
                e.drain()
                e.iota(ioti[:].rearrange("p (w t) -> p w t", t=32),
                       [[1, 16], [16, 32]], base=0, channel_multiplier=-1)
                e.drain()
                e.tensor_copy(iota_sw[:], ioti[:])
                e.memset(ones_bf[:], 1.0)
                e.memset(ones_row[:], 1.0)
                e.memset(one128[:], 1.0)
                return e.drain()
            s.add("gpsimd", g_iota, outs=["iota", "ones", "ident"])

            s.add("scalar", lambda e: e.activation(amp_sb[:], amp_sb[:], AF.Square),
                  ins=["amp"], outs=["amp2q"])
            s.add("tensor", lambda e: e.matmul(psa[:], ones_row[:], esc_row[:],
                                               start=True, stop=True),
                  ins=["escr", "ones"], outs=["psa0"])
            s.add("tensor", lambda e: e.matmul(psb[:], ones_row[:], nd_row[:],
                                               start=True, stop=True),
                  ins=["ndr", "ones"], outs=["psb0"])

            def v_mix(e):
                e.tensor_copy(esc_bc[:], psa[:])
                e.tensor_copy(nd_bc[:], psb[:])
                e.tensor_tensor(out=p8[:], in0=amp_sb[:, 0:16:2],
                                in1=amp_sb[:, 1:16:2], op=ALU.add)
                e.drain()
                e.reduce_sum(r8[:], p8[:], axis=AX.X)
                e.drain()
                e.tensor_scalar(out=r8[:], in0=r8[:], scalar1=float(EPS_R),
                                scalar2=None, op0=ALU.add)
                e.drain()
                e.reciprocal(r8[:], r8[:])
                e.drain()
                e.tensor_scalar(out=p8[:], in0=p8[:], scalar1=r8[:, 0:1],
                                scalar2=None, op0=ALU.mult)
                e.drain()
                ins = None
                for k in range(8):
                    ins = e.tensor_scalar(
                        out=mixM[:, k * 128:(k + 1) * 128], in0=one128[:],
                        scalar1=p8[:, k:k + 1], scalar2=None, op0=ALU.mult)
                return ins
            s.add("vector", v_mix, ins=["psa0", "psb0", "amp2q", "ones"],
                  outs=["escbc", "ndbc", "mix"])
            s.flush()

    def emit_ln_tile(s, src_of_c, dst_key, dst_of_c, bufs, src_keys, W=QT,
                     su=0):
        xbf, tmp, rows, ps_s, ps_q, bcab = bufs

        def K(n):
            return (n, su)

        def RR(i):
            return rows[:1, i * W:(i + 1) * W]

        def v_cast(e):
            ins = None
            for c in range(NCH):
                ins = e.tensor_copy(xbf[:, c * W:(c + 1) * W], src_of_c(c))
            return ins
        s.add("vector", v_cast, ins=list(src_keys), outs=[K("ln_xbf")])

        def t_sum(e):
            ins = None
            for c in range(NCH):
                ins = e.matmul(ps_s, ones_bf[:], xbf[:, c * W:(c + 1) * W],
                               start=(c == 0), stop=(c == NCH - 1))
            return ins
        s.add("tensor", t_sum, ins=[K("ln_xbf"), "ones"], outs=[K("ln_pss")])

        def sc_sq(e):
            ins = None
            for c in range(NCH):
                ins = e.activation(xbf[:, c * W:(c + 1) * W], src_of_c(c),
                                   AF.Square)
            return ins
        s.add("scalar", sc_sq, ins=list(src_keys) + [K("ln_pss")],
              outs=[K("ln_xbf")])

        def t_sq(e):
            ins = None
            for c in range(NCH):
                ins = e.matmul(ps_q, ones_bf[:], xbf[:, c * W:(c + 1) * W],
                               start=(c == 0), stop=(c == NCH - 1))
            return ins
        s.add("tensor", t_sq, ins=[K("ln_xbf")], outs=[K("ln_psq")])

        def v_stats(e):
            e.tensor_scalar(out=RR(0), in0=ps_s, scalar1=float(1.0 / D),
                            scalar2=None, op0=ALU.mult)
            e.tensor_scalar(out=RR(1), in0=ps_q, scalar1=float(1.0 / D),
                            scalar2=None, op0=ALU.mult)
            return e.drain()
        s.add("vector", v_stats, ins=[K("ln_pss"), K("ln_psq")],
              outs=[K("ln_r01")])

        s.add("scalar", lambda e: e.activation(RR(2), RR(0), AF.Square),
              ins=[K("ln_r01")], outs=[K("ln_r2")])

        def v_var(e):
            e.tensor_tensor(out=RR(1), in0=RR(1), in1=RR(2), op=ALU.subtract)
            return e.drain()
        s.add("vector", v_var, ins=[K("ln_r2"), K("ln_r01")], outs=[K("ln_var")])

        s.add("scalar", lambda e: e.activation(RR(2), RR(1), AF.Sqrt,
                                               bias=np.float32(EPS_LN)),
              ins=[K("ln_var"), K("ln_r2")], outs=[K("ln_std")])

        def v_ab(e):
            e.reciprocal(RR(3), RR(2))
            e.drain()
            e.scalar_tensor_tensor(out=RR(4), in0=RR(0), scalar=-1.0, in1=RR(3),
                                   op0=ALU.mult, op1=ALU.mult)
            return e.drain()
        s.add("vector", v_ab, ins=[K("ln_std"), K("ln_r01")], outs=[K("ln_ab")])

        s.add("scalar", lambda e: e.dma_start(
            ln_scr[su:su + 1, 0:2 * W], rows[:1, 3 * W:5 * W]),
            ins=[K("ln_ab")], outs=[K("ln_scr")], dma=dA)

        def d_bc(e):
            srcp = ln_scr[su:su + 1, 0:2 * W]
            bsrc = bass.AP(srcp.tensor, srcp.offset,
                           [[0, 128]] + list(srcp.ap[1:]))
            return e.dma_start(bcab[:], bsrc)
        s.add("scalar", d_bc, ins=[K("ln_scr")], outs=[K("ln_bc")], dma=dA)

        def v_norm(e):
            ins = None
            for c in range(NCH):
                e.tensor_tensor(out=tmp[:], in0=src_of_c(c),
                                in1=bcab[:, 0:W], op=ALU.mult)
                ins = e.tensor_tensor(out=dst_of_c(c), in0=tmp[:],
                                      in1=bcab[:, W:2 * W], op=ALU.add)
            return ins
        s.add("vector", v_norm, ins=[K("ln_bc")] + list(src_keys),
              outs=[dst_key])

    def emit_phase_A0bq():
        LW = 256
        with contextlib.ExitStack() as _st:
            EC = _st.enter_context
            xlnO = EC(nc.sbuf_tensor(_u("xlnO"), [128, NCH * TOWN], BF16))
            xtile2 = [EC(nc.sbuf_tensor(_u(f"lnq_xt{i}"), [128, NCH * LW],
                                        F32)) for i in range(2)]
            lnb = []
            for i in range(2):
                lnb.append((
                    EC(nc.sbuf_tensor(_u(f"lnq_xbf{i}"), [128, NCH * LW],
                                      BF16)),
                    EC(nc.sbuf_tensor(_u(f"lnq_tmp{i}"), [128, LW], F32)),
                    EC(nc.sbuf_tensor(_u(f"lnq_rows{i}"), [1, 5 * LW], F32)),
                    EC(nc.psum_tensor(_u(f"lnq_ps_s{i}"), [1, LW], F32)),
                    EC(nc.psum_tensor(_u(f"lnq_ps_q{i}"), [1, LW], F32)),
                    EC(nc.sbuf_tensor(_u(f"lnq_bc{i}"), [128, 2 * LW], F32)),
                ))
            wqb = [EC(nc.sbuf_tensor(_u(f"a1qb_w{i}"), [128, NCH * 128], BF16))
                   for i in range(6)]
            qstage = EC(nc.sbuf_tensor(_u("a1qb_qs"), [128, 4 * 2048], BF16))
            ps0 = EC(nc.psum_tensor(_u("a1qb_ps0"), [128, QT], F32))
            ps1 = EC(nc.psum_tensor(_u("a1qb_ps1"), [128, QT], F32))
            pss = (ps0, ps1)
            s = Sched()

            def emit_lno(j):
                su = j % 2
                xt = xtile2[su]
                xbf, tmp, rows, ps_s, ps_q, bcab = lnb[su]
                s.add("sync" if su == 0 else "gpsimd",
                      lambda e, j=j, xt=xt: e.dma_start(
                          xt[:].rearrange("p (c t) -> p c t", t=LW),
                          xQown.rearrange("(c p) t -> p c t", p=128)
                          [:, :, j * LW:(j + 1) * LW]),
                      outs=[("ln_xtile", su)],
                      dma=dS if su == 0 else dG)
                emit_ln_tile(
                    s,
                    src_of_c=lambda c, xt=xt: xt[:, c * LW:(c + 1) * LW],
                    dst_key=("xlnO", j),
                    dst_of_c=lambda c, j=j: xlnO[:, c * TOWN + j * LW:
                                                 c * TOWN + j * LW + LW],
                    bufs=(xbf, tmp, rows, ps_s[:1, :], ps_q[:1, :], bcab),
                    src_keys=[("ln_xtile", su)], W=LW, su=su)

            it = 0
            emit_lno(0)
            for hg in range(4):
                emit_lno(2 * hg + 1)
                if hg < 3:
                    emit_lno(2 * hg + 2)
                for w in range(16):
                    wk = ("qw", w % 6)
                    s.add("gpsimd", lambda e, w=w: e.dma_start(
                        wqb[w % 6][:].rearrange("p (c j) -> p c j", j=128),
                        Wq.rearrange("(c p) j -> p c j", p=128)
                        [:, :, w * 128:(w + 1) * 128]), outs=[wk], dma=dG)
                    pi = it % 2

                    def t_mm(e, hg=hg, w=w, pi=pi):
                        ins = None
                        for c in range(NCH):
                            ins = e.matmul(
                                pss[pi][:],
                                wqb[w % 6][:, c * 128:(c + 1) * 128],
                                xlnO[:, c * TOWN + hg * 512:
                                     c * TOWN + hg * 512 + 512],
                                start=(c == 0), stop=(c == NCH - 1))
                        return ins
                    s.add("tensor", t_mm,
                          ins=[wk, ("xlnO", 2 * hg), ("xlnO", 2 * hg + 1)],
                          outs=[("qps", pi)])

                    def sc_ev(e, w=w, pi=pi):
                        ins = None
                        for hl in range(4):
                            src_ap = pss[pi][:, hl * 128:(hl + 1) * 128] \
                                .rearrange("p (s t) -> p s t", t=32)
                            dst = qstage[:, hl * 2048:(hl + 1) * 2048] \
                                .rearrange("p (s q) -> p s q", q=512) \
                                [:, :, w * 32:w * 32 + 32]
                            ins = e.activation(dst, src_ap, AF.Identity,
                                               bias=bq_cols[:, w:w + 1])
                        return ins
                    s.add("scalar", sc_ev, ins=[("qps", pi), "bq"],
                          outs=[("qstage", w)])
                    it += 1
                for hl in range(4):
                    H = hg * 4 + hl
                    s.add("sync", lambda e, H=H, hl=hl: e.dma_start(
                        q_scr[:, H * TOWN:(H + 1) * TOWN],
                        qstage[:, hl * 2048:(hl + 1) * 2048]),
                        ins=[("qstage", w_) for w_ in range(16)],
                        outs=["q_scr"], dma=dS)
            s.add("sync", lambda e: None, extra=[(dS, dS.n)])
            s.flush()

    LNW = 256
    NLT = SEQ // LNW

    def emit_phase_A0k(xlnP):
        with contextlib.ExitStack() as _st:
            EC = _st.enter_context
            xtile2 = [EC(nc.sbuf_tensor(_u(f"lnp_xt{i}"), [128, NCH * LNW],
                                        F32)) for i in range(2)]
            lnb = []
            for i in range(2):
                lnb.append((
                    EC(nc.sbuf_tensor(_u(f"lnp_xbf{i}"), [128, NCH * LNW],
                                      BF16)),
                    EC(nc.sbuf_tensor(_u(f"lnp_tmp{i}"), [128, LNW], F32)),
                    EC(nc.sbuf_tensor(_u(f"lnp_rows{i}"), [1, 5 * LNW], F32)),
                    EC(nc.psum_tensor(_u(f"lnp_ps_s{i}"), [1, LNW], F32)),
                    EC(nc.psum_tensor(_u(f"lnp_ps_q{i}"), [1, LNW], F32)),
                    EC(nc.sbuf_tensor(_u(f"lnp_bc{i}"), [128, 2 * LNW], F32)),
                ))
            w0 = EC(nc.sbuf_tensor(_u("a1k_w0"), [128, NCH * 128], BF16))
            w1 = EC(nc.sbuf_tensor(_u("a1k_w1"), [128, NCH * 128], BF16))
            ob0 = EC(nc.sbuf_tensor(_u("a1k_ob0"), [128, 512], BF16))
            ob1 = EC(nc.sbuf_tensor(_u("a1k_ob1"), [128, 512], BF16))
            ps0 = EC(nc.psum_tensor(_u("a1k_ps0"), [128, 512], F32))
            ps1 = EC(nc.psum_tensor(_u("a1k_ps1"), [128, 512], F32))
            wsb = (w0, w1); obs = (ob0, ob1); pss = (ps0, ps1)
            s = Sched()
            it = [0]
            loaded = set()

            def load_w(sl):
                if sl in loaded:
                    return
                loaded.add(sl)
                s.add("gpsimd", lambda e, sl=sl: e.dma_start(
                    wsb[sl % 2][:].rearrange("p (c j) -> p c j", j=128),
                    Wk.rearrange("(c p) j -> p c j", p=128)
                    [:, :, sl * 128:(sl + 1) * 128]),
                    outs=[("kw", sl % 2)], dma=dG)

            def emit_k(sl, tt):
                load_w(sl)
                tkeys = [("xlnP", 2 * tt + i) for i in range(2)]
                h = sl
                pi = it[0] % 2

                def t_mm(e, sl=sl, tt=tt, pi=pi):
                    ins = None
                    for c in range(NCH):
                        ins = e.matmul(
                            pss[pi][:],
                            wsb[sl % 2][:, c * 128:(c + 1) * 128],
                            xlnP[:, c * SEQ + tt * 512:
                                 c * SEQ + tt * 512 + 512],
                            start=(c == 0), stop=(c == NCH - 1))
                    return ins
                s.add("tensor", t_mm, ins=[("kw", sl % 2)] + tkeys,
                      outs=[("ps", pi)])

                s.add("scalar", lambda e, h=h, pi=pi: e.activation(
                    obs[pi][:], pss[pi][:], AF.Identity,
                    bias=bk_cols[:, h:h + 1]),
                    ins=[("ps", pi), "bk"], outs=[("ob", pi)])

                s.add("sync", lambda e, h=h, tt=tt, pi=pi: e.dma_start(
                    k_scr[:, h * SEQ + tt * 512:h * SEQ + tt * 512 + 512],
                    obs[pi][:]), ins=[("ob", pi)], outs=["k_scr"], dma=dS)
                it[0] += 1

            load_w(0)
            for j in range(NLT):
                su = j % 2
                xt = xtile2[su]
                xbf, tmp, rows, ps_s, ps_q, bcab = lnb[su]
                s.add("sync" if su == 0 else "gpsimd",
                      lambda e, j=j, xt=xt: e.dma_start(
                          xt[:].rearrange("p (c t) -> p c t", t=LNW),
                          xT.rearrange("(c p) t -> p c t", p=128)
                          [:, :, j * LNW:(j + 1) * LNW]),
                      outs=[("ln_xtile2", su)],
                      dma=dS if su == 0 else dG)
                emit_ln_tile(
                    s,
                    src_of_c=lambda c, xt=xt: xt[:, c * LNW:(c + 1) * LNW],
                    dst_key=("xlnP", j),
                    dst_of_c=lambda c, j=j: xlnP[:, c * SEQ + j * LNW:
                                                 c * SEQ + j * LNW + LNW],
                    bufs=(xbf, tmp, rows, ps_s[:1, :], ps_q[:1, :], bcab),
                    src_keys=[("ln_xtile2", su)], W=LNW, su=su)
                if j % 2 == 1:
                    emit_k(0, j // 2)
            for sl in range(1, 16):
                for tt in range(8):
                    emit_k(sl, tt)
            s.add("sync", lambda e: None, extra=[(dS, dS.n)])
            s.flush()

    def emit_proj_v(xln_sb, xln_keys):
        with (
            nc.sbuf_tensor(_u("a1v_w"), [128, NCH * 1024], BF16) as wv_sb,
            nc.sbuf_tensor(_u("a1v_bvr"), [1, D], F32) as bv_row,
            nc.sbuf_tensor(_u("a1v_bvbc"), [128, 1024], F32) as bv_bc,
            nc.sbuf_tensor(_u("a1v_ob0"), [128, 1024], BF16) as ob0,
            nc.sbuf_tensor(_u("a1v_ob1"), [128, 1024], BF16) as ob1,
            nc.psum_tensor(_u("a1v_ps0"), [128, 1024], F32) as ps0,
            nc.psum_tensor(_u("a1v_ps1"), [128, 1024], F32) as ps1,
            nc.psum_tensor(_u("a1v_psbv"), [128, 1024], F32) as psbv,
        ):
            obs = (ob0, ob1); pss = (ps0, ps1)
            s = Sched()
            s.add("sync", lambda e: e.dma_start(
                bv_row[:], bv.rearrange("(o d) -> o d", o=1)), outs=["bvr"], dma=dS)
            for hg in range(2):
                s.add("gpsimd", lambda e, hg=hg: e.dma_start(
                    wv_sb[:].rearrange("p (c j) -> p c j", j=1024),
                    Wv.rearrange("(c p) j -> p c j", p=128)
                    [:, :, hg * 1024:(hg + 1) * 1024]), outs=["wv"], dma=dG)

                def t_bv(e, hg=hg):
                    e.matmul(psbv[:, 0:512], ones_row[:],
                             bv_row[:1, hg * 1024:hg * 1024 + 512],
                             start=True, stop=True)
                    return e.matmul(psbv[:, 512:1024], ones_row[:],
                                    bv_row[:1, hg * 1024 + 512:(hg + 1) * 1024],
                                    start=True, stop=True)
                s.add("tensor", t_bv, ins=["bvr", "ones"], outs=["psbv"])
                s.add("vector", lambda e: e.tensor_copy(bv_bc[:], psbv[:]),
                      ins=["psbv"], outs=["bvbc"])

                for tb in range(32):
                    pi = tb % 2

                    def t_mm(e, tb=tb, pi=pi):
                        ins = None
                        for c in range(NCH):
                            ins = _mm2(
                                e, pss[pi][:],
                                xln_sb[:, c * SEQ + tb * 128:c * SEQ + tb * 128 + 128],
                                wv_sb[:, c * 1024:(c + 1) * 1024],
                                start=(c == 0), stop=(c == NCH - 1))
                        return ins
                    s.add("tensor", t_mm, ins=["wv"] + list(xln_keys),
                          outs=[("vps", pi)])

                    s.add("vector", lambda e, pi=pi: e.tensor_tensor(
                        out=obs[pi][:], in0=pss[pi][:], in1=bv_bc[:], op=ALU.add),
                        ins=[("vps", pi), "bvbc"], outs=[("vob", pi)])

                    s.add("sync", lambda e, tb=tb, hg=hg, pi=pi: e.dma_start(
                        v_scr[:, tb * 2048 + hg * 1024:tb * 2048 + hg * 1024 + 1024],
                        obs[pi][:]), ins=[("vob", pi)], outs=["v_scr"], dma=dS)
            s.add("sync", lambda e: None, extra=[(dS, dS.n)])
            s.flush()

    def emit_attn(attnout_sb):
        with contextlib.ExitStack() as _st:
            EC = _st.enter_context
            masks = EC(nc.sbuf_tensor(_u("a2_masks"), [128, 32 * QT], BF16))
            kT = EC(nc.sbuf_tensor(_u("a2_kT"), [128, GH * SEQ], BF16))
            qTt = EC(nc.sbuf_tensor(_u("a2_qT"), [128, GH * TOWN], BF16))
            vsb = EC(nc.sbuf_tensor(_u("a2_v"), [128, 32 * GH * 128], BF16))
            exb = [[EC(nc.sbuf_tensor(_u(f"a2_ex{st}{j}"), [128, 1024], BF16))
                    for j in range(3)] for st in range(2)]
            red = [EC(nc.sbuf_tensor(_u(f"a2_red{st}"), [128, QT], BF16))
                   for st in range(2)]
            stk = [EC(nc.sbuf_tensor(_u(f"a2_stk{st}"), [128, 2 * QT], BF16))
                   for st in range(2)]
            rrow2 = EC(nc.sbuf_tensor(_u("a2_rrow"), [33, QT], F32))
            rrow = [rrow2[0:1, :], rrow2[32:33, :]]
            avout = [EC(nc.sbuf_tensor(_u(f"a2_avo{st}"), [128, QT], BF16))
                     for st in range(2)]
            bcsb = [EC(nc.sbuf_tensor(_u(f"a2_bcsb{st}"), [128, QT], F32))
                    for st in range(2)]
            scb = [EC(nc.psum_tensor(_u(f"a2_sc{st}"), [128, 1024], F32))
                   for st in range(2)]
            avb = [EC(nc.psum_tensor(_u(f"a2_av{st}"), [128, QT], F32))
                   for st in range(2)]
            pssum = [EC(nc.psum_tensor(_u(f"a2_sum{st}"), [1, QT], F32))
                     for st in range(2)]
            s = Sched()

            def v_masks(e):
                for site in range(32):
                    e.tensor_scalar(
                        out=masks[:, site * QT:(site + 1) * QT], in0=iota_sw[:],
                        scalar1=nd_bc[:, site:site + 1], scalar2=None,
                        op0=ALU.is_ge)
                e.drain()
                ins = None
                for site in range(32):
                    ins = e.tensor_scalar(
                        out=masks[:, site * QT:(site + 1) * QT],
                        in0=masks[:, site * QT:(site + 1) * QT],
                        scalar1=1.0, scalar2=10000.0,
                        op0=ALU.subtract, op1=ALU.mult)
                return ins
            s.add("vector", v_masks, ins=["iota", "ndbc"], outs=["masks"])

            pend_t2 = [None, None]

            def emit_pair(g, st, hl, qi, u, npair):
                masked = u >= npair - 4
                pi = u % 3

                def t_sc(e, hl=hl, qi=qi, u=u, st=st, masked=masked,
                         npair=npair):
                    ins = None
                    for i in range(2):
                        kb = 2 * u + i
                        ins = e.matmul(
                            scb[st][:, i * QT:(i + 1) * QT],
                            kT[:, hl * SEQ + kb * 128:hl * SEQ + kb * 128 + 128],
                            qTt[:, hl * TOWN + qi * QT:hl * TOWN + (qi + 1) * QT],
                            start=True, stop=not masked)
                        if masked:
                            site = qi * 8 + (u - (npair - 4)) * 2 + i
                            ins = e.matmul(
                                scb[st][:, i * QT:(i + 1) * QT],
                                ident_bf[:],
                                masks[:, site * QT:(site + 1) * QT],
                                start=False, stop=True)
                    return ins
                s.add("tensor", t_sc,
                      ins=["a2k", "a2q"] + (["masks", "ident"] if masked
                                            else []),
                      outs=[("sc", st)])

                s.add("scalar", lambda e, st=st, pi=pi: e.activation(
                    exb[st][pi][:], scb[st][:], AF.Exp, scale=SCALE),
                    ins=[("sc", st)], outs=[("ex", st, pi)])

                if u % 2 == 0:
                    def v_red(e, st=st, pi=pi):
                        return e.tensor_tensor(
                            out=red[st][:], in0=exb[st][pi][:, 0:QT],
                            in1=exb[st][pi][:, QT:2 * QT], op=ALU.add)
                    s.add("vector", v_red, ins=[("ex", st, pi)],
                          outs=[("red", st)])
                else:
                    rg = (u // 2) % 2

                    def v_s1(e, st=st, pi=pi, rg=rg):
                        return e.tensor_tensor(
                            out=stk[st][:, rg * QT:(rg + 1) * QT],
                            in0=red[st][:], in1=exb[st][pi][:, 0:QT],
                            op=ALU.add)
                    s.add("vector", v_s1, ins=[("ex", st, pi), ("red", st)],
                          outs=[("stkp", st, rg), ("stk", st, rg)])

                    def v_s2(e, st=st, pi=pi, rg=rg):
                        return e.tensor_tensor(
                            out=stk[st][:, rg * QT:(rg + 1) * QT],
                            in0=stk[st][:, rg * QT:(rg + 1) * QT],
                            in1=exb[st][pi][:, QT:2 * QT], op=ALU.add)
                    s.add("vector", v_s2, ins=[("ex", st, pi), ("stkp", st, rg)],
                          outs=[("stk", st, rg)])
                if u == 3 and pend_t2[st] is not None:
                    pend_t2[st]()
                    pend_t2[st] = None
                if u >= 3 and u % 2 == 1:
                    emit_sum(st, (u - 3) // 2, first=(u == 3), last=False)
                if u >= 2:
                    emit_av(st, hl, qi, u - 2, npair)

            def emit_sum(st, j, first, last):
                rg = j % 2

                def t_sum(e, st=st, rg=rg, first=first, last=last):
                    return e.matmul(pssum[st][:1, :], ones_bf[:],
                                    stk[st][:, rg * QT:(rg + 1) * QT],
                                    start=first, stop=last)
                s.add("tensor", t_sum, ins=[("stk", st, rg), "ones"],
                      outs=[("sum", st)])

            def emit_av(st, hl, qi, u, npair):
                def t_av(e, st=st, hl=hl, qi=qi, u=u, npair=npair):
                    ins = None
                    for i in range(2):
                        kb = 2 * u + i
                        ins = e.matmul(
                            avb[st][:],
                            vsb[:, kb * 512 + hl * 128:kb * 512 + hl * 128 + 128],
                            exb[st][u % 3][:, i * QT:(i + 1) * QT],
                            start=(u == 0 and i == 0),
                            stop=(u == npair - 1 and i == 1))
                    return ins
                s.add("tensor", t_av, ins=[("ex", st, u % 3), "a2v"],
                      outs=[("av", st)])

            def emit_tile_end(g, st, hl, qi, npair):
                emit_av(st, hl, qi, npair - 2, npair)
                emit_av(st, hl, qi, npair - 1, npair)
                emit_sum(st, npair // 2 - 1, first=(npair == 2), last=True)

                h = g * GH + hl

                def v_ev(e, st=st):
                    return e.tensor_copy(avout[st][:], avb[st][:])
                s.add("vector", v_ev, ins=[("av", st)], outs=[("avo", st)])

                def v_t1(e, st=st):
                    return e.reciprocal(rrow[st], pssum[st][:1, :])
                s.add("vector", v_t1, ins=[("sum", st)], outs=[("rrow", st)])
                s.add("sync", lambda e, st=st: e.dma_start(
                    nrm_scr[st:st + 1, :], rrow[st]),
                    ins=[("rrow", st)], outs=[("nscr", st)], dma=dS)

                def d_bc(e, st=st):
                    src = nrm_scr[st:st + 1, :]
                    bsrc = bass.AP(src.tensor, src.offset,
                                   [[0, 128]] + list(src.ap[1:]))
                    return e.dma_start(bcsb[st][:], bsrc)
                s.add("sync", d_bc, ins=[("nscr", st)], outs=[("bcsb", st)],
                      dma=dS)

                def emit_t2(st=st, h=h, qi=qi):
                    def v_t2(e, st=st, h=h, qi=qi):
                        return e.tensor_tensor(
                            out=attnout_sb[:, h * TOWN + qi * QT:
                                           h * TOWN + (qi + 1) * QT],
                            in0=avout[st][:], in1=bcsb[st][:], op=ALU.mult)
                    s.add("vector", v_t2, ins=[("bcsb", st), ("avo", st)],
                          outs=["attnout"])
                pend_t2[st] = emit_t2

            for g in range(NG):
                s.add("sync", lambda e, g=g: e.dma_start(
                    kT[:], k_scr[:, g * GH * SEQ:(g + 1) * GH * SEQ]),
                    ins=["k_scr"], outs=["a2k"], dma=dS)
                s.add("sync", lambda e, g=g: e.dma_start(
                    qTt[:], q_scr[:, g * GH * TOWN:(g + 1) * GH * TOWN]),
                    ins=["q_scr"], outs=["a2q"], dma=dS)
                for tb in range(32):
                    s.add("sync", lambda e, g=g, tb=tb: e.dma_start(
                        vsb[:, tb * 512:(tb + 1) * 512],
                        v_scr[:, tb * 2048 + g * 512:tb * 2048 + g * 512 + 512]),
                        ins=["v_scr"], outs=["a2v"], dma=dS)

                def events(st):
                    ev = []
                    for hl in (2 * st, 2 * st + 1):
                        for qi in range(NQT):
                            npair = KBMAX[qi] // 2
                            for u in range(npair):
                                ev.append(("pair", st, hl, qi, u, npair))
                            ev.append(("end", st, hl, qi, npair))
                    return ev
                evA, evB = events(0), events(1)
                assert len(evA) == len(evB)
                for ea, eb in zip(evA, evB):
                    for kind, st, hl, qi, x, *rest in (ea, eb):
                        if kind == "pair":
                            emit_pair(g, st, hl, qi, x, rest[0])
                        else:
                            emit_tile_end(g, st, hl, qi, x)
            for st in range(2):
                if pend_t2[st] is not None:
                    pend_t2[st]()
                    pend_t2[st] = None
            s.flush()

    def emit_wo(attnout_sb):
        with (
            nc.sbuf_tensor(_u("a3_wo"), [128, NH * D], BF16) as wo_sb,
            nc.sbuf_tensor(_u("a3_xt"), [128, NCH * QT], F32) as xt,
            nc.sbuf_tensor(_u("a3_ob0"), [128, QT], F32) as ob0,
            nc.sbuf_tensor(_u("a3_ob1"), [128, QT], F32) as ob1,
            nc.sbuf_tensor(_u("a3_x20"), [128, QT], F32) as x20,
            nc.sbuf_tensor(_u("a3_x21"), [128, QT], F32) as x21,
            nc.psum_tensor(_u("a3_ps0"), [128, QT], F32) as ps0,
            nc.psum_tensor(_u("a3_ps1"), [128, QT], F32) as ps1,
        ):
            obs = (ob0, ob1); x2s = (x20, x21); pss = (ps0, ps1)
            s = Sched()
            for j in range(16):
                s.add("gpsimd", lambda e, j=j: e.dma_start(
                    wo_sb[:, j * 2048:(j + 1) * 2048]
                    .rearrange("p (h jj) -> p h jj", jj=128),
                    Wo.rearrange("(h p) jj -> p h jj", p=128)
                    [:, :, j * 128:(j + 1) * 128]), outs=["wo"], dma=dG)
            it = 0
            for qt4 in range(4):
                s.add("sync", lambda e, qt4=qt4: e.dma_start(
                    xt[:].rearrange("p (c t) -> p c t", t=QT),
                    xTown.rearrange("(c p) t -> p c t", p=128)
                    [:, :, qt4 * QT:(qt4 + 1) * QT]), outs=["a3xt"], dma=dS)
                for j in range(16):
                    pi = it % 2

                    def t_mm(e, j=j, qt4=qt4, pi=pi):
                        ins = None
                        for h in range(NH):
                            ins = e.matmul(
                                pss[pi][:],
                                wo_sb[:, j * 2048 + h * 128:j * 2048 + h * 128 + 128],
                                attnout_sb[:, h * TOWN + qt4 * QT:
                                           h * TOWN + (qt4 + 1) * QT],
                                start=(h == 0), stop=(h == NH - 1))
                        return ins
                    s.add("tensor", t_mm, ins=["wo", "attnout"], outs=[("wops", pi)])

                    s.add("scalar", lambda e, j=j, pi=pi: e.activation(
                        obs[pi][:], pss[pi][:], AF.Identity, bias=bo_cols[:, j:j + 1]),
                        ins=[("wops", pi), "bo"], outs=[("wob", pi)])

                    s.add("vector", lambda e, j=j, pi=pi: e.tensor_tensor(
                        out=x2s[pi][:], in0=obs[pi][:],
                        in1=xt[:, j * QT:(j + 1) * QT], op=ALU.add),
                        ins=[("wob", pi), "a3xt"], outs=[("x2b", pi)])

                    s.add("sync", lambda e, j=j, qt4=qt4, pi=pi: e.dma_start(
                        x2_scr[:, j * TOWN + qt4 * QT:j * TOWN + qt4 * QT + QT],
                        x2s[pi][:]), ins=[("x2b", pi)], outs=["x2_scr"], dma=dS)
                    it += 1
            s.add("sync", lambda e: None, extra=[(dS, dS.n)])
            s.flush()

    def emit_ffn():
        with (
            nc.sbuf_tensor(_u("b_xln"), [128, NCH * FT], BF16) as xln2,
            nc.sbuf_tensor(_u("b_abc"), [128, 8 * FT], BF16) as abc,
            nc.sbuf_tensor(_u("b_wacc"), [128, 8 * FT], F32) as wacc,
            nc.sbuf_tensor(_u("b_smid"), [128, NCH * FT], BF16) as smid,
        ):
            for tt in range(NFT):
                LW2 = 256
                with contextlib.ExitStack() as _st:
                    EC = _st.enter_context
                    x2t2 = [EC(nc.sbuf_tensor(_u(f"bl_x2t{i}"),
                                              [128, NCH * LW2], F32))
                            for i in range(2)]
                    lnb = []
                    for i in range(2):
                        lnb.append((
                            EC(nc.sbuf_tensor(_u(f"bl_xbf{i}"),
                                              [128, NCH * LW2], BF16)),
                            EC(nc.sbuf_tensor(_u(f"bl_tmp{i}"), [128, LW2],
                                              F32)),
                            EC(nc.sbuf_tensor(_u(f"bl_rows{i}"), [1, 5 * LW2],
                                              F32)),
                            EC(nc.psum_tensor(_u(f"bl_ps_s{i}"), [1, LW2],
                                              F32)),
                            EC(nc.psum_tensor(_u(f"bl_ps_q{i}"), [1, LW2],
                                              F32)),
                            EC(nc.sbuf_tensor(_u(f"bl_bc{i}"), [128, 2 * LW2],
                                              F32)),
                        ))
                    s = Sched()
                    for sub in range(4):
                        su = sub % 2
                        base = tt * FT + sub * LW2
                        xt = x2t2[su]
                        xbf, tmp, rows, ps_s, ps_q, bcab = lnb[su]
                        s.add("sync" if su == 0 else "gpsimd",
                              lambda e, base=base, xt=xt: e.dma_start(
                                  xt[:].rearrange("p (c t) -> p c t", t=LW2),
                                  x2_scr.rearrange("p (c t) -> p c t", c=NCH)
                                  [:, :, base:base + LW2]),
                              ins=["x2_scr"], outs=[("bx2t", su)],
                              dma=dS if su == 0 else dG)
                        emit_ln_tile(
                            s,
                            src_of_c=lambda c, xt=xt: xt[:, c * LW2:
                                                         (c + 1) * LW2],
                            dst_key="xln2" if sub == 3 else ("xln2s", sub),
                            dst_of_c=lambda c, sub=sub: xln2[
                                :, c * FT + sub * LW2:c * FT + sub * LW2 + LW2],
                            bufs=(xbf, tmp, rows, ps_s[:1, :], ps_q[:1, :],
                                  bcab),
                            src_keys=[("bx2t", su)], W=LW2, su=su)
                    s.flush()
                if tt == 0 and "xln2" in tap_outs:
                    s = Sched()
                    s.add("sync", lambda e: e.dma_start(tap_outs["xln2"], xln2[:]),
                          ins=["xln2"], dma=dS)
                    s.add("sync", lambda e: None, extra=[(dS, dS.n)])
                    s.flush()

                with (
                    nc.sbuf_tensor(_u("br_wr"), [128, NCH * 128], BF16) as wr_sb,
                    nc.sbuf_tensor(_u("br_e"), [128, 128], F32) as e_sb,
                    nc.sbuf_tensor(_u("br_rows"), [128, 12], F32) as rws,
                    nc.sbuf_tensor(_u("br_mbf"), [128, 128], BF16) as mbf,
                    nc.sbuf_tensor(_u("br_mT"), [128, 8 * 128], BF16) as mT,
                    nc.psum_tensor(_u("br_psr"), [128, 128], F32) as psr,
                    nc.psum_tensor(_u("br_psT"), [128, 128], BF16) as psT,
                    nc.psum_tensor(_u("br_psal"), [128, QT], F32) as psal,
                    nc.psum_tensor(_u("br_psbc"), [128, QT], F32) as psbc2,
                ):
                    s = Sched()
                    s.add("gpsimd", lambda e: e.dma_start(
                        wr_sb[:].rearrange("p (c j) -> p c j", j=128),
                        w_router.rearrange("(c p) j -> p c j", p=128)),
                        outs=["wr"], dma=dG)
                    for sub in range(8):
                        def t_r(e, sub=sub):
                            ins = None
                            for c in range(NCH):
                                ins = e.matmul(
                                    psr[:],
                                    xln2[:, c * FT + sub * 128:c * FT + sub * 128 + 128],
                                    wr_sb[:, c * 128:(c + 1) * 128],
                                    start=(c == 0), stop=(c == NCH - 1))
                            return ins
                        s.add("tensor", t_r, ins=["wr", "xln2"], outs=["psr"])

                        def v_max(e):
                            e.reduce_max(rws[:, 0:1], psr[:], axis=AX.X)
                            e.drain()
                            e.tensor_scalar(out=rws[:, 1:2], in0=rws[:, 0:1],
                                            scalar1=-1.0, scalar2=None, op0=ALU.mult)
                            return e.drain()
                        s.add("vector", v_max, ins=["psr"], outs=["nmax"])

                        s.add("scalar", lambda e: e.activation(
                            e_sb[:], psr[:], AF.Exp, bias=rws[:, 1:2],
                            accum_out=rws[:, 2:3]),
                            ins=["psr", "nmax"], outs=["esb", "zrow"])

                        def v_top4(e):
                            e.max(out=rws[:, 4:12], in_=e_sb[:])
                            e.drain()
                            e.tensor_scalar(out=mbf[:], in0=e_sb[:],
                                            scalar1=rws[:, 7:8], scalar2=None,
                                            op0=ALU.is_ge)
                            e.tensor_tensor(out=e_sb[:], in0=e_sb[:], in1=mbf[:],
                                            op=ALU.mult)
                            e.drain()
                            e.reduce_sum(rws[:, 3:4], e_sb[:], axis=AX.X)
                            e.tensor_scalar(out=rws[:, 4:5], in0=rws[:, 2:3],
                                            scalar1=float(EPS_R), scalar2=None,
                                            op0=ALU.mult)
                            e.drain()
                            e.tensor_tensor(out=rws[:, 3:4], in0=rws[:, 3:4],
                                            in1=rws[:, 4:5], op=ALU.add)
                            e.drain()
                            e.reciprocal(rws[:, 3:4], rws[:, 3:4])
                            e.drain()
                            e.tensor_scalar(out=e_sb[:], in0=e_sb[:],
                                            scalar1=rws[:, 3:4], scalar2=None,
                                            op0=ALU.mult)
                            return e.tensor_tensor(out=mbf[:], in0=e_sb[:],
                                                   in1=esc_bc[:], op=ALU.mult)
                        s.add("vector", v_top4, ins=["esb", "zrow", "escbc"],
                              outs=["mbf"])

                        s.add("tensor", lambda e: e.transpose(psT[:], mbf[:],
                                                              ident_bf[:]),
                              ins=["mbf", "ident"], outs=["psT"])
                        s.add("vector", lambda e, sub=sub: e.tensor_copy(
                            mT[:, sub * 128:(sub + 1) * 128], psT[:]),
                            ins=["psT"], outs=["mT"])

                    alps = (psal, psbc2)
                    for kh in range(16):
                        k, half = kh // 2, kh % 2
                        pa = kh % 2
                        s.add("tensor", lambda e, k=k, half=half, pa=pa: e.matmul(
                            alps[pa][:], mixM[:, k * 128:(k + 1) * 128],
                            mT[:, half * QT:(half + 1) * QT],
                            start=True, stop=True),
                            ins=["mT", "mix"], outs=[("psal", pa)])
                        s.add("vector", lambda e, k=k, half=half, pa=pa:
                              e.tensor_copy(
                                  abc[:, k * FT + half * QT:
                                      k * FT + half * QT + QT],
                                  alps[pa][:]), ins=[("psal", pa)], outs=["abc"])
                    if tt == 0 and "mT" in tap_outs:
                        s.add("sync", lambda e: e.dma_start(tap_outs["mT"], mT[:]),
                              ins=["mT"], dma=dS)
                        s.add("sync", lambda e: e.dma_start(tap_outs["abc"], abc[:]),
                              ins=["abc"], dma=dS)
                        s.add("sync", lambda e: None, extra=[(dS, dS.n)])
                    s.flush()

                with (
                    nc.sbuf_tensor(_u("b2_wg0"), [128, NCH * 128], BF16) as wg0,
                    nc.sbuf_tensor(_u("b2_wg1"), [128, NCH * 128], BF16) as wg1,
                    nc.sbuf_tensor(_u("b2_wu0"), [128, NCH * 128], BF16) as wu0,
                    nc.sbuf_tensor(_u("b2_wu1"), [128, NCH * 128], BF16) as wu1,
                    nc.sbuf_tensor(_u("b2_sil0"), [128, FT], F32) as sil0,
                    nc.sbuf_tensor(_u("b2_sil1"), [128, FT], F32) as sil1,
                    nc.sbuf_tensor(_u("b2_bas0"), [128, FT], BF16) as bas0,
                    nc.sbuf_tensor(_u("b2_bas1"), [128, FT], BF16) as bas1,
                    nc.sbuf_tensor(_u("b2_bs0"), [128, FT], BF16) as bs0,
                    nc.sbuf_tensor(_u("b2_bs1"), [128, FT], BF16) as bs1,
                    nc.psum_tensor(_u("b2_psA"), [128, FT], F32) as psA,
                    nc.psum_tensor(_u("b2_psB"), [128, FT], F32) as psB,
                    nc.psum_tensor(_u("b2_psC"), [128, FT], F32) as psC,
                ):
                    wgs = (wg0, wg1); wus = (wu0, wu1); sils = (sil0, sil1)
                    bas = (bas0, bas1); bss = (bs0, bs1); gps = (psA, psB)
                    s = Sched()
                    for kic in range(64):
                        k, ic = kic // 8, kic % 8
                        pi = kic % 2
                        s.add("gpsimd", lambda e, kic=kic, pi=pi: e.dma_start(
                            wgs[pi][:].rearrange("p (c j) -> p c j", j=128),
                            w_gate.rearrange("(c p) j -> p c j", p=128)
                            [:, :, kic * 128:(kic + 1) * 128]),
                            outs=[("wg", pi)], dma=dG)
                        s.add("gpsimd", lambda e, kic=kic, pi=pi: e.dma_start(
                            wus[pi][:].rearrange("p (c j) -> p c j", j=128),
                            w_up.rearrange("(c p) j -> p c j", p=128)
                            [:, :, kic * 128:(kic + 1) * 128]),
                            outs=[("wu", pi)], dma=dG)

                        def t_g(e, pi=pi):
                            ins = None
                            for c in range(NCH):
                                ins = _mm2(
                                    e, gps[pi][:], wgs[pi][:, c * 128:(c + 1) * 128],
                                    xln2[:, c * FT:(c + 1) * FT],
                                    start=(c == 0), stop=(c == NCH - 1))
                            return ins
                        s.add("tensor", t_g, ins=[("wg", pi), "xln2"],
                              outs=[("gps", pi)])

                        def t_u(e, pi=pi):
                            ins = None
                            for c in range(NCH):
                                ins = _mm2(
                                    e, psC[:], wus[pi][:, c * 128:(c + 1) * 128],
                                    xln2[:, c * FT:(c + 1) * FT],
                                    start=(c == 0), stop=(c == NCH - 1))
                            return ins
                        s.add("tensor", t_u, ins=[("wu", pi), "xln2"], outs=["psC"])

                        s.add("scalar", lambda e, pi=pi: e.activation(
                            sils[pi][:], gps[pi][:], AF.Silu),
                            ins=[("gps", pi)], outs=[("sil", pi)])

                        s.add("vector", lambda e, pi=pi: e.tensor_tensor(
                            out=bas[pi][:], in0=sils[pi][:], in1=psC[:], op=ALU.mult),
                            ins=[("sil", pi), "psC"], outs=[("bas", pi)])

                        s.add("vector", lambda e, k=k, pi=pi: e.tensor_tensor(
                            out=bss[pi][:], in0=bas[pi][:],
                            in1=abc[:, k * FT:(k + 1) * FT], op=ALU.mult),
                            ins=[("bas", pi), "abc"], outs=[("bs", pi)])

                        if k == 0:
                            s.add("vector", lambda e, ic=ic, pi=pi: e.tensor_copy(
                                wacc[:, ic * FT:(ic + 1) * FT], bss[pi][:]),
                                ins=[("bs", pi)], outs=[("wacc", ic)])
                        else:
                            def v_acc(e, ic=ic, pi=pi):
                                return e.tensor_tensor(
                                    out=wacc[:, ic * FT:(ic + 1) * FT],
                                    in0=wacc[:, ic * FT:(ic + 1) * FT],
                                    in1=bss[pi][:], op=ALU.add)
                            s.add("vector", v_acc, ins=[("bs", pi), ("wacc", ic)],
                                  outs=[("wacc", ic)])
                    if tt == 0 and "wacc" in tap_outs:
                        s.add("sync", lambda e: e.dma_start(tap_outs["wacc"], wacc[:]),
                              ins=[("wacc", ic) for ic in range(8)], dma=dS)
                        s.add("sync", lambda e: None, extra=[(dS, dS.n)])
                    s.flush()

                with (
                    nc.sbuf_tensor(_u("b3_wg0"), [128, NCH * 128], BF16) as swg0,
                    nc.sbuf_tensor(_u("b3_wg1"), [128, NCH * 128], BF16) as swg1,
                    nc.sbuf_tensor(_u("b3_wu0"), [128, NCH * 128], BF16) as swu0,
                    nc.sbuf_tensor(_u("b3_wu1"), [128, NCH * 128], BF16) as swu1,
                    nc.sbuf_tensor(_u("b3_sil0"), [128, FT], F32) as ssil0,
                    nc.sbuf_tensor(_u("b3_sil1"), [128, FT], F32) as ssil1,
                    nc.psum_tensor(_u("b3_psA"), [128, FT], F32) as psA3,
                    nc.psum_tensor(_u("b3_psB"), [128, FT], F32) as psB3,
                    nc.psum_tensor(_u("b3_psC"), [128, FT], F32) as psC3,
                ):
                    wgs = (swg0, swg1); wus = (swu0, swu1); sils = (ssil0, ssil1)
                    gps = (psA3, psB3)
                    s = Sched()
                    for oc in range(16):
                        pi = oc % 2
                        s.add("gpsimd", lambda e, oc=oc, pi=pi: e.dma_start(
                            wgs[pi][:].rearrange("p (c j) -> p c j", j=128),
                            swg.rearrange("(c p) j -> p c j", p=128)
                            [:, :, oc * 128:(oc + 1) * 128]),
                            outs=[("swg", pi)], dma=dG)
                        s.add("gpsimd", lambda e, oc=oc, pi=pi: e.dma_start(
                            wus[pi][:].rearrange("p (c j) -> p c j", j=128),
                            swu.rearrange("(c p) j -> p c j", p=128)
                            [:, :, oc * 128:(oc + 1) * 128]),
                            outs=[("swu", pi)], dma=dG)

                        def t_g(e, pi=pi):
                            ins = None
                            for c in range(NCH):
                                ins = _mm2(
                                    e, gps[pi][:], wgs[pi][:, c * 128:(c + 1) * 128],
                                    xln2[:, c * FT:(c + 1) * FT],
                                    start=(c == 0), stop=(c == NCH - 1))
                            return ins
                        s.add("tensor", t_g, ins=[("swg", pi), "xln2"],
                              outs=[("sgps", pi)])

                        def t_u(e, pi=pi):
                            ins = None
                            for c in range(NCH):
                                ins = _mm2(
                                    e, psC3[:], wus[pi][:, c * 128:(c + 1) * 128],
                                    xln2[:, c * FT:(c + 1) * FT],
                                    start=(c == 0), stop=(c == NCH - 1))
                            return ins
                        s.add("tensor", t_u, ins=[("swu", pi), "xln2"], outs=["psC3"])

                        s.add("scalar", lambda e, pi=pi: e.activation(
                            sils[pi][:], gps[pi][:], AF.Silu),
                            ins=[("sgps", pi)], outs=[("ssil", pi)])

                        s.add("vector", lambda e, oc=oc, pi=pi: e.tensor_tensor(
                            out=smid[:, oc * FT:(oc + 1) * FT], in0=sils[pi][:],
                            in1=psC3[:], op=ALU.mult),
                            ins=[("ssil", pi), "psC3"], outs=["smid"])
                    if tt == 0 and "smid" in tap_outs:
                        s.add("sync", lambda e: e.dma_start(tap_outs["smid"], smid[:]),
                              ins=["smid"], dma=dS)
                        s.add("sync", lambda e: None, extra=[(dS, dS.n)])
                    s.flush()

                with (
                    nc.sbuf_tensor(_u("b4_wbf"), [128, 8 * FT], BF16) as wbf,
                    nc.sbuf_tensor(_u("b4_wd0"), [128, 8 * 128], BF16) as wd0,
                    nc.sbuf_tensor(_u("b4_wd1"), [128, 8 * 128], BF16) as wd1,
                    nc.sbuf_tensor(_u("b4_wsd0"), [128, NCH * 128], BF16) as wsd0,
                    nc.sbuf_tensor(_u("b4_wsd1"), [128, NCH * 128], BF16) as wsd1,
                    nc.sbuf_tensor(_u("b4_x20"), [128, FT], F32) as x20,
                    nc.sbuf_tensor(_u("b4_x21"), [128, FT], F32) as x21,
                    nc.sbuf_tensor(_u("b4_yb0"), [128, FT], F32) as yb0,
                    nc.sbuf_tensor(_u("b4_yb1"), [128, FT], F32) as yb1,
                    nc.psum_tensor(_u("b4_ps0"), [128, FT], F32) as bps0,
                    nc.psum_tensor(_u("b4_ps1"), [128, FT], F32) as bps1,
                ):
                    wds = (wd0, wd1); wsds = (wsd0, wsd1)
                    x2s = (x20, x21); ybs = (yb0, yb1); pss = (bps0, bps1)
                    s = Sched()

                    def v_cast(e):
                        return e.tensor_copy(wbf[:], wacc[:])
                    s.add("vector", v_cast,
                          ins=[("wacc", ic) for ic in range(8)], outs=["wbf"])

                    for j in range(16):
                        pi = j % 2
                        s.add("gpsimd", lambda e, j=j, pi=pi: e.dma_start(
                            wds[pi][:].rearrange("p (c jj) -> p c jj", jj=128),
                            w_down.rearrange("(c p) jj -> p c jj", p=128)
                            [:, :, j * 128:(j + 1) * 128]),
                            outs=[("wd", pi)], dma=dG)
                        s.add("gpsimd", lambda e, j=j, pi=pi: e.dma_start(
                            wsds[pi][:].rearrange("p (c jj) -> p c jj", jj=128),
                            swd.rearrange("(c p) jj -> p c jj", p=128)
                            [:, :, j * 128:(j + 1) * 128]),
                            outs=[("wsd", pi)], dma=dG)
                        s.add("sync", lambda e, j=j, tt=tt, pi=pi: e.dma_start(
                            x2s[pi][:],
                            x2_scr[:, j * TOWN + tt * FT:j * TOWN + tt * FT + FT]),
                            ins=["x2_scr"], outs=[("bx2", pi)], dma=dS)

                        def t_mm(e, j=j, pi=pi):
                            ins = None
                            for ic in range(8):
                                _mm2(e, pss[pi][:],
                                     wds[pi][:, ic * 128:(ic + 1) * 128],
                                     wbf[:, ic * FT:(ic + 1) * FT],
                                     start=(ic == 0), stop=False)
                            for oc in range(NCH):
                                ins = _mm2(e, pss[pi][:],
                                           wsds[pi][:, oc * 128:(oc + 1) * 128],
                                           smid[:, oc * FT:(oc + 1) * FT],
                                           start=False, stop=(oc == NCH - 1))
                            return ins
                        s.add("tensor", t_mm,
                              ins=[("wd", pi), ("wsd", pi), "wbf", "smid"],
                              outs=[("bps", pi)])

                        s.add("vector", lambda e, pi=pi: e.tensor_tensor(
                            out=ybs[pi][:], in0=pss[pi][:], in1=x2s[pi][:],
                            op=ALU.add),
                            ins=[("bps", pi), ("bx2", pi)], outs=[("yb", pi)])

                        s.add("sync", lambda e, j=j, tt=tt, pi=pi: e.dma_start(
                            yT[j * 128:(j + 1) * 128, tt * FT:(tt + 1) * FT],
                            ybs[pi][:]), ins=[("yb", pi)], outs=["yT"], dma=dS)
                    s.add("sync", lambda e: None, extra=[(dS, dS.n)])
                    s.flush()

    def tap(name, src):
        if name in tap_outs:
            s = Sched()
            s.add("sync", lambda e: e.dma_start(tap_outs[name], src), dma=dS)
            s.add("sync", lambda e: None, extra=[(dS, dS.n)])
            s.flush()

    if "A0b" in phases or "A1q" in phases:
        emit_phase_A0bq()
    if "A0" in phases or "A1k" in phases or "A1v" in phases:
        with nc.sbuf_tensor(_u("xlnP"), [128, NCH * SEQ], BF16) as xlnP:
            emit_phase_A0k(xlnP)
            if "A1v" in phases:
                emit_proj_v(xlnP, [("xlnP", j) for j in range(NLT)])
    tap("q_scr", q_scr)
    tap("k_scr", k_scr)
    tap("v_scr", v_scr)
    if "A2" in phases or "A3" in phases:
        with nc.sbuf_tensor(_u("attnout"), [128, NH * TOWN], BF16) as attnout_sb:
            if "A2" in phases:
                emit_attn(attnout_sb)
                tap("attnout", attnout_sb[:])
            if "A3" in phases:
                emit_wo(attnout_sb)
    tap("x2", x2_scr)
    if "B" in phases:
        emit_ffn()

    s = Sched()
    s.add("sync", lambda e: None, extra=[(dS, dS.n), (dG, dG.n)])
    s.add("gpsimd", lambda e: None, extra=[(dG, dG.n)])
    s.flush()
    return nc


def _scram_cols(par):
    own = OWN_BLOCKS[par]
    jj = np.arange(QT)
    off = (jj % 32) * 16 + jj // 32
    return np.concatenate([blk * QT + off for blk in own])


def _qsrc_cols(par):
    own = OWN_BLOCKS[par]
    cols = []
    for H in range(16):
        for blk in own:
            cols.append(H * 256 + blk * 32 + np.arange(32))
    return np.concatenate(cols)


def _core_inputs(inputs, core):
    b = core // 2
    par = core % 2
    x = inputs["L0_attn_x"]
    xT_b = np.ascontiguousarray(x[b].T)
    own = OWN_BLOCKS[par]
    cols = _scram_cols(par)
    xTown = np.ascontiguousarray(xT_b[:, cols])
    xQown = np.ascontiguousarray(xT_b[:, _qsrc_cols(par)])
    nd = np.zeros((1, 32), np.float32)
    for qi in range(NQT):
        B = own[qi] * QT
        for r in range(8):
            kb = KBMAX[qi] - 8 + r
            nd[0, qi * 8 + r] = 128 * kb - B
    return {
        "xT": xT_b, "xTown": xTown, "xQown": xQown, "negdelta": nd,
        "Wq": inputs["L0_attn_Wq"], "Wk": inputs["L0_attn_Wk"],
        "Wv": inputs["L0_attn_Wv"], "Wo": inputs["L0_attn_Wo"],
        "bq": inputs["L0_attn_bq"], "bk": inputs["L0_attn_bk"],
        "bv": inputs["L0_attn_bv"], "bo": inputs["L0_attn_bo"],
        "w_router": inputs["L0_ffn_w_router"],
        "w_gate": inputs["L0_ffn_w_gate"], "w_up": inputs["L0_ffn_w_up"],
        "w_down": inputs["L0_ffn_w_down"],
        "amp2": np.ascontiguousarray(
            inputs["L0_ffn_amplitudes"].reshape(128, 16)),
        "esc": np.ascontiguousarray(
            inputs["L0_ffn_expert_scale"].reshape(1, 128)),
        "swg": inputs["L0_ffn_shared_w_gate"],
        "swu": inputs["L0_ffn_shared_w_up"],
        "swd": inputs["L0_ffn_shared_w_down"],
    }


def kernel(**inputs):
    nc = build_program()
    in_maps = [
        {k: np.ascontiguousarray(np.asarray(v, dtype=np.float32))
         for k, v in _core_inputs(inputs, c).items()}
        for c in range(8)
    ]
    res = run_bass_kernel_spmd(nc, in_maps, core_ids=list(range(8)))
    out = np.empty((4, SEQ, D), np.float32)
    for c in range(8):
        b = c // 2
        cols = _scram_cols(c % 2)
        yT_c = res.results[c]["yT"]
        out[b, cols, :] = yT_c.T
    return out


# version 18
# speedup vs baseline: 1.0533x; 1.0533x over previous
import contextlib

import numpy as np
import concourse.bass as bass
import concourse.mybir as mybir
from concourse.bass_utils import run_bass_kernel_spmd

F32 = mybir.dt.float32
BF16 = mybir.dt.bfloat16
I32 = mybir.dt.int32
AF = mybir.ActivationFunctionType
ALU = mybir.AluOpType
AX = mybir.AxisListType

D = 2048
NCH = 16
SEQ = 4096
TOWN = 2048
QT = 512
NQT = 4
KBMAX = (8, 16, 24, 32)
NH = 16
DH = 128
GH = 4
NG = 4
SCALE = float(1.0 / np.sqrt(128.0))
EPS_LN = 1e-5
EPS_R = 1e-8
FT = 1024
NFT = 2
OWN_BLOCKS = ((0, 3, 4, 7), (1, 2, 5, 6))
ENGS = ("sync", "tensor", "scalar", "vector", "gpsimd")


def _mm2(e, out_ap, lhsT, rhs, start, stop):
    n = rhs.shape[-1]
    ins = None
    for off in range(0, n, 512):
        w = min(512, n - off)
        ins = e.matmul(out_ap[:, off:off + w], lhsT, rhs[:, off:off + w],
                       start=start, stop=stop)
    return ins


ALL_PHASES = ("W0", "A0b", "A1q", "A0", "A1k", "A1v", "A2", "A3", "B")


def build_program(phases=ALL_PHASES, taps=(), ffn_taps=False):
    nc = bass.Bass("TRN2", target_bir_lowering=False, debug=False)

    def din(name, shape):
        return nc.dram_tensor(name, shape, F32, kind="ExternalInput").ap()

    xT = din("xT", [16 * 128, 16 * 256])
    xTown = din("xTown", [D, TOWN])
    xQown = din("xQown", [8 * 128, 16 * 256])
    negdelta = din("negdelta", [1, 32])
    Wq = din("Wq", [D, D]); Wk = din("Wk", [D, D]); Wv = din("Wv", [D, D])
    Wo = din("Wo", [D, D])
    bq = din("bq", [D]); bk = din("bk", [D]); bv = din("bv", [D]); bo = din("bo", [D])
    w_router = din("w_router", [D, 128])
    w_gate = din("w_gate", [D, 8192]); w_up = din("w_up", [D, 8192])
    w_down = din("w_down", [1024, D])
    amp2 = din("amp2", [128, 16]); esc = din("esc", [1, 128])
    swg = din("swg", [D, D]); swu = din("swu", [D, D]); swd = din("swd", [D, D])
    yT = nc.dram_tensor("yT", [D, TOWN], F32, kind="ExternalOutput").ap()

    q_scr = nc.dram_tensor("q_scr", [128, NH * TOWN], BF16, kind="Internal").ap()
    k_scr = nc.dram_tensor("k_scr", [128, NH * SEQ], BF16, kind="Internal").ap()
    v_scr = nc.dram_tensor("v_scr", [128, 32 * D], BF16, kind="Internal").ap()
    x2_scr = nc.dram_tensor("x2_scr", [128, NCH * TOWN], F32, kind="Internal").ap()
    nrm_scr = nc.dram_tensor("nrm_scr", [2, QT], F32, kind="Internal").ap()
    ln_scr = nc.dram_tensor("ln_scr", [2, 2 * QT], F32, kind="Internal").ap()

    tap_outs = {}
    if ffn_taps:
        for t, shape, dt in (("xln2", [128, NCH * FT], BF16),
                             ("mT", [128, 8 * 128], BF16),
                             ("abc", [128, 8 * FT], BF16),
                             ("wacc", [128, 8 * FT], F32),
                             ("smid", [128, NCH * FT], BF16)):
            tap_outs[t] = nc.dram_tensor("tap_" + t, shape, dt,
                                         kind="ExternalOutput").ap()
    for t in taps:
        shape, dt = {
            "xlnO": ([128, NCH * TOWN], BF16), "xlnP": ([128, NCH * SEQ], BF16),
            "q_scr": ([128, NH * TOWN], BF16), "k_scr": ([128, NH * SEQ], BF16),
            "v_scr": ([128, 32 * D], BF16), "attnout": ([128, NH * TOWN], BF16),
            "x2": ([128, NCH * TOWN], F32)}[t]
        tap_outs[t] = nc.dram_tensor("tap_" + t, shape, dt, kind="ExternalOutput").ap()


    _uid_ctr = [0]
    def _u(name):
        _uid_ctr[0] += 1
        return f"{name}_{_uid_ctr[0]}"

    class Ct:
        __slots__ = ("h", "n", "eng")
        def __init__(self, name, eng):
            self.h = nc.alloc_semaphore(name=name)
            self.n = 0
            self.eng = eng

    dS = Ct("dS", None)
    dG = Ct("dG", None)
    dA = Ct("dA", None)
    cts = {"tensor": Ct("sPE", "tensor"), "scalar": Ct("sSC", "scalar"),
           "vector": Ct("sDV", "vector"), "gpsimd": Ct("sPO", "gpsimd"),
           "sync": Ct("sSY", "sync")}

    ready = {}
    free = {}

    class Sched:
        def __init__(self):
            self.ev = {e: [] for e in ENGS}

        def add(self, eng, fn, ins=(), outs=(), extra=(), dma=None):
            waits = {}

            def addw(w):
                prev = waits.get(id(w[0]))
                waits[id(w[0])] = (w[0], max(w[1], prev[1] if prev else 0))

            for k in ins:
                w = ready.get(k)
                if w:
                    addw(w)
            for k in outs:
                for w in free.get(k, ()):
                    addw(w)
                w = ready.get(k)
                if w:
                    addw(w)
            for w in extra:
                addw(w)
            if dma is not None:
                ct, delta = dma, 16
            else:
                ct, delta = cts[eng], 1
            ct.n += delta
            self.ev[eng].append((fn, [(c, v) for c, v in waits.values()], ct, delta))
            mark = (ct, ct.n)
            for k in outs:
                ready[k] = mark
                free[k] = []
            for k in ins:
                free.setdefault(k, []).append(mark)

        def flush(self):
            with nc.Block() as blk:
                for eng in ENGS:
                    evs = self.ev[eng]
                    if not evs:
                        continue

                    def body(e, evs=evs):
                        for fn, waits, ct, delta in evs:
                            for c, v in waits:
                                e.wait_ge(c.h, v)
                            ins = fn(e)
                            if ins is None:
                                ins = e.nop()
                            ins.then_inc(ct.h, delta)

                    getattr(blk, eng)(body)
            self.ev = {e: [] for e in ENGS}

    ones_bf = nc.alloc_sbuf_tensor("c_ones_bf", [128, 1], BF16)
    ones_row = nc.alloc_sbuf_tensor("c_ones_row", [1, 128], F32)
    one128 = nc.alloc_sbuf_tensor("c_one128", [128, 128], BF16)
    iota_sw = nc.alloc_sbuf_tensor("c_iota_sw", [128, QT], F32)
    nd_bc = nc.alloc_sbuf_tensor("c_nd_bc", [128, 32], F32)
    ident_bf = nc.alloc_sbuf_tensor("c_ident", [128, 128], BF16)
    bq_cols = nc.alloc_sbuf_tensor("c_bq", [128, 16], F32)
    bk_cols = nc.alloc_sbuf_tensor("c_bk", [128, 16], F32)
    bo_cols = nc.alloc_sbuf_tensor("c_bo", [128, 16], F32)
    esc_bc = nc.alloc_sbuf_tensor("c_esc_bc", [128, 128], F32)
    mixM = nc.alloc_sbuf_tensor("c_mixM", [128, 8 * 128], BF16)

    if "W0" in phases:
        with (
            nc.sbuf_tensor(_u("w0_ioti"), [128, QT], I32) as ioti,
            nc.sbuf_tensor(_u("w0_amp"), [128, 16], F32) as amp_sb,
            nc.sbuf_tensor(_u("w0_p8"), [128, 8], F32) as p8,
            nc.sbuf_tensor(_u("w0_r8"), [128, 1], F32) as r8,
            nc.sbuf_tensor(_u("w0_esc_row"), [1, 128], F32) as esc_row,
            nc.sbuf_tensor(_u("w0_nd_row"), [1, 32], F32) as nd_row,
            nc.psum_tensor(_u("w0_psa"), [128, 128], F32) as psa,
            nc.psum_tensor(_u("w0_psb"), [128, 32], F32) as psb,
        ):
            s = Sched()
            s.add("sync", lambda e: e.dma_start(amp_sb[:], amp2), outs=["amp"], dma=dS)
            s.add("sync", lambda e: e.dma_start(esc_row[:], esc), outs=["escr"], dma=dS)
            s.add("sync", lambda e: e.dma_start(nd_row[:], negdelta), outs=["ndr"], dma=dS)
            for c in range(16):
                for nm, col, src in (("bq", bq_cols, bq), ("bk", bk_cols, bk),
                                     ("bo", bo_cols, bo)):
                    s.add("sync", lambda e, col=col, src=src, c=c: e.dma_start(
                        col[:, c:c + 1],
                        src[c * 128:(c + 1) * 128].rearrange("(p o) -> p o", o=1)),
                        outs=[nm], dma=dS)

            def g_iota(e):
                e.iota(ioti[:], [[1, QT]], base=0, channel_multiplier=-1)
                e.drain()
                e.tensor_scalar(out=ident_bf[:], in0=ioti[:, 0:128],
                                scalar1=0, scalar2=None, op0=ALU.is_equal)
                e.drain()
                e.iota(ioti[:].rearrange("p (w t) -> p w t", t=32),
                       [[1, 16], [16, 32]], base=0, channel_multiplier=-1)
                e.drain()
                e.tensor_copy(iota_sw[:], ioti[:])
                e.memset(ones_bf[:], 1.0)
                e.memset(ones_row[:], 1.0)
                e.memset(one128[:], 1.0)
                return e.drain()
            s.add("gpsimd", g_iota, outs=["iota", "ones", "ident"])

            s.add("scalar", lambda e: e.activation(amp_sb[:], amp_sb[:], AF.Square),
                  ins=["amp"], outs=["amp2q"])
            s.add("tensor", lambda e: e.matmul(psa[:], ones_row[:], esc_row[:],
                                               start=True, stop=True),
                  ins=["escr", "ones"], outs=["psa0"])
            s.add("tensor", lambda e: e.matmul(psb[:], ones_row[:], nd_row[:],
                                               start=True, stop=True),
                  ins=["ndr", "ones"], outs=["psb0"])

            def v_mix(e):
                e.tensor_copy(esc_bc[:], psa[:])
                e.tensor_copy(nd_bc[:], psb[:])
                e.tensor_tensor(out=p8[:], in0=amp_sb[:, 0:16:2],
                                in1=amp_sb[:, 1:16:2], op=ALU.add)
                e.drain()
                e.reduce_sum(r8[:], p8[:], axis=AX.X)
                e.drain()
                e.tensor_scalar(out=r8[:], in0=r8[:], scalar1=float(EPS_R),
                                scalar2=None, op0=ALU.add)
                e.drain()
                e.reciprocal(r8[:], r8[:])
                e.drain()
                e.tensor_scalar(out=p8[:], in0=p8[:], scalar1=r8[:, 0:1],
                                scalar2=None, op0=ALU.mult)
                e.drain()
                ins = None
                for k in range(8):
                    ins = e.tensor_scalar(
                        out=mixM[:, k * 128:(k + 1) * 128], in0=one128[:],
                        scalar1=p8[:, k:k + 1], scalar2=None, op0=ALU.mult)
                return ins
            s.add("vector", v_mix, ins=["psa0", "psb0", "amp2q", "ones"],
                  outs=["escbc", "ndbc", "mix"])
            s.flush()

    def emit_ln_tile(s, src_of_c, dst_key, dst_of_c, bufs, src_keys, W=QT,
                     su=0):
        xbf, tmp, rows, ps_s, ps_q, bcab = bufs

        def K(n):
            return (n, su)

        def RR(i):
            return rows[:1, i * W:(i + 1) * W]

        def v_cast(e):
            ins = None
            for c in range(NCH):
                ins = e.tensor_copy(xbf[:, c * W:(c + 1) * W], src_of_c(c))
            return ins
        s.add("vector", v_cast, ins=list(src_keys), outs=[K("ln_xbf")])

        def t_sum(e):
            ins = None
            for c in range(NCH):
                ins = e.matmul(ps_s, ones_bf[:], xbf[:, c * W:(c + 1) * W],
                               start=(c == 0), stop=(c == NCH - 1))
            return ins
        s.add("tensor", t_sum, ins=[K("ln_xbf"), "ones"], outs=[K("ln_pss")])

        def sc_sq(e):
            ins = None
            for c in range(NCH):
                ins = e.activation(xbf[:, c * W:(c + 1) * W], src_of_c(c),
                                   AF.Square)
            return ins
        s.add("scalar", sc_sq, ins=list(src_keys) + [K("ln_pss")],
              outs=[K("ln_xbf")])

        def t_sq(e):
            ins = None
            for c in range(NCH):
                ins = e.matmul(ps_q, ones_bf[:], xbf[:, c * W:(c + 1) * W],
                               start=(c == 0), stop=(c == NCH - 1))
            return ins
        s.add("tensor", t_sq, ins=[K("ln_xbf")], outs=[K("ln_psq")])

        def v_stats(e):
            e.tensor_scalar(out=RR(0), in0=ps_s, scalar1=float(1.0 / D),
                            scalar2=None, op0=ALU.mult)
            e.tensor_scalar(out=RR(1), in0=ps_q, scalar1=float(1.0 / D),
                            scalar2=None, op0=ALU.mult)
            return e.drain()
        s.add("vector", v_stats, ins=[K("ln_pss"), K("ln_psq")],
              outs=[K("ln_r01")])

        s.add("scalar", lambda e: e.activation(RR(2), RR(0), AF.Square),
              ins=[K("ln_r01")], outs=[K("ln_r2")])

        def v_var(e):
            e.tensor_tensor(out=RR(1), in0=RR(1), in1=RR(2), op=ALU.subtract)
            return e.drain()
        s.add("vector", v_var, ins=[K("ln_r2"), K("ln_r01")], outs=[K("ln_var")])

        s.add("scalar", lambda e: e.activation(RR(2), RR(1), AF.Sqrt,
                                               bias=np.float32(EPS_LN)),
              ins=[K("ln_var"), K("ln_r2")], outs=[K("ln_std")])

        def v_ab(e):
            e.reciprocal(RR(3), RR(2))
            e.drain()
            e.scalar_tensor_tensor(out=RR(4), in0=RR(0), scalar=-1.0, in1=RR(3),
                                   op0=ALU.mult, op1=ALU.mult)
            return e.drain()
        s.add("vector", v_ab, ins=[K("ln_std"), K("ln_r01")], outs=[K("ln_ab")])

        s.add("scalar", lambda e: e.dma_start(
            ln_scr[su:su + 1, 0:2 * W], rows[:1, 3 * W:5 * W]),
            ins=[K("ln_ab")], outs=[K("ln_scr")], dma=dA)

        def d_bc(e):
            srcp = ln_scr[su:su + 1, 0:2 * W]
            bsrc = bass.AP(srcp.tensor, srcp.offset,
                           [[0, 128]] + list(srcp.ap[1:]))
            return e.dma_start(bcab[:], bsrc)
        s.add("scalar", d_bc, ins=[K("ln_scr")], outs=[K("ln_bc")], dma=dA)

        def v_norm(e):
            ins = None
            for c in range(NCH):
                e.tensor_tensor(out=tmp[:], in0=src_of_c(c),
                                in1=bcab[:, 0:W], op=ALU.mult)
                ins = e.tensor_tensor(out=dst_of_c(c), in0=tmp[:],
                                      in1=bcab[:, W:2 * W], op=ALU.add)
            return ins
        s.add("vector", v_norm, ins=[K("ln_bc")] + list(src_keys),
              outs=[dst_key])

    def emit_phase_A0bq():
        LW = 256
        with contextlib.ExitStack() as _st:
            EC = _st.enter_context
            xlnO = EC(nc.sbuf_tensor(_u("xlnO"), [128, NCH * TOWN], BF16))
            xtile2 = [EC(nc.sbuf_tensor(_u(f"lnq_xt{i}"), [128, NCH * LW],
                                        F32)) for i in range(2)]
            lnb = []
            for i in range(2):
                lnb.append((
                    EC(nc.sbuf_tensor(_u(f"lnq_xbf{i}"), [128, NCH * LW],
                                      BF16)),
                    EC(nc.sbuf_tensor(_u(f"lnq_tmp{i}"), [128, LW], F32)),
                    EC(nc.sbuf_tensor(_u(f"lnq_rows{i}"), [1, 5 * LW], F32)),
                    EC(nc.psum_tensor(_u(f"lnq_ps_s{i}"), [1, LW], F32)),
                    EC(nc.psum_tensor(_u(f"lnq_ps_q{i}"), [1, LW], F32)),
                    EC(nc.sbuf_tensor(_u(f"lnq_bc{i}"), [128, 2 * LW], F32)),
                ))
            wqb = [EC(nc.sbuf_tensor(_u(f"a1qb_w{i}"), [128, NCH * 128], BF16))
                   for i in range(6)]
            qstage = EC(nc.sbuf_tensor(_u("a1qb_qs"), [128, 4 * 2048], BF16))
            ps0 = EC(nc.psum_tensor(_u("a1qb_ps0"), [128, QT], F32))
            ps1 = EC(nc.psum_tensor(_u("a1qb_ps1"), [128, QT], F32))
            pss = (ps0, ps1)
            s = Sched()

            def emit_lno(j):
                su = j % 2
                xt = xtile2[su]
                xbf, tmp, rows, ps_s, ps_q, bcab = lnb[su]
                s.add("sync" if su == 0 else "gpsimd",
                      lambda e, j=j, xt=xt: e.dma_start(
                          xt[:], xQown[j * 128:(j + 1) * 128, :]),
                      outs=[("ln_xtile", su)],
                      dma=dS if su == 0 else dG)
                emit_ln_tile(
                    s,
                    src_of_c=lambda c, xt=xt: xt[:, c * LW:(c + 1) * LW],
                    dst_key=("xlnO", j),
                    dst_of_c=lambda c, j=j: xlnO[:, c * TOWN + j * LW:
                                                 c * TOWN + j * LW + LW],
                    bufs=(xbf, tmp, rows, ps_s[:1, :], ps_q[:1, :], bcab),
                    src_keys=[("ln_xtile", su)], W=LW, su=su)

            it = 0
            emit_lno(0)
            for hg in range(4):
                emit_lno(2 * hg + 1)
                if hg < 3:
                    emit_lno(2 * hg + 2)
                for w in range(16):
                    wk = ("qw", w % 6)
                    s.add("gpsimd", lambda e, w=w: e.dma_start(
                        wqb[w % 6][:].rearrange("p (c j) -> p c j", j=128),
                        Wq.rearrange("(c p) j -> p c j", p=128)
                        [:, :, w * 128:(w + 1) * 128]), outs=[wk], dma=dG)
                    pi = it % 2

                    def t_mm(e, hg=hg, w=w, pi=pi):
                        ins = None
                        for c in range(NCH):
                            ins = e.matmul(
                                pss[pi][:],
                                wqb[w % 6][:, c * 128:(c + 1) * 128],
                                xlnO[:, c * TOWN + hg * 512:
                                     c * TOWN + hg * 512 + 512],
                                start=(c == 0), stop=(c == NCH - 1))
                        return ins
                    s.add("tensor", t_mm,
                          ins=[wk, ("xlnO", 2 * hg), ("xlnO", 2 * hg + 1)],
                          outs=[("qps", pi)])

                    def sc_ev(e, w=w, pi=pi):
                        ins = None
                        for hl in range(4):
                            src_ap = pss[pi][:, hl * 128:(hl + 1) * 128] \
                                .rearrange("p (s t) -> p s t", t=32)
                            dst = qstage[:, hl * 2048:(hl + 1) * 2048] \
                                .rearrange("p (s q) -> p s q", q=512) \
                                [:, :, w * 32:w * 32 + 32]
                            ins = e.activation(dst, src_ap, AF.Identity,
                                               bias=bq_cols[:, w:w + 1])
                        return ins
                    s.add("scalar", sc_ev, ins=[("qps", pi), "bq"],
                          outs=[("qstage", w)])
                    it += 1
                for hl in range(4):
                    H = hg * 4 + hl
                    s.add("sync", lambda e, H=H, hl=hl: e.dma_start(
                        q_scr[:, H * TOWN:(H + 1) * TOWN],
                        qstage[:, hl * 2048:(hl + 1) * 2048]),
                        ins=[("qstage", w_) for w_ in range(16)],
                        outs=["q_scr"], dma=dS)
            s.add("sync", lambda e: None, extra=[(dS, dS.n)])
            s.flush()

    LNW = 256
    NLT = SEQ // LNW

    def emit_phase_A0k(xlnP):
        with contextlib.ExitStack() as _st:
            EC = _st.enter_context
            xtile2 = [EC(nc.sbuf_tensor(_u(f"lnp_xt{i}"), [128, NCH * LNW],
                                        F32)) for i in range(2)]
            lnb = []
            for i in range(2):
                lnb.append((
                    EC(nc.sbuf_tensor(_u(f"lnp_xbf{i}"), [128, NCH * LNW],
                                      BF16)),
                    EC(nc.sbuf_tensor(_u(f"lnp_tmp{i}"), [128, LNW], F32)),
                    EC(nc.sbuf_tensor(_u(f"lnp_rows{i}"), [1, 5 * LNW], F32)),
                    EC(nc.psum_tensor(_u(f"lnp_ps_s{i}"), [1, LNW], F32)),
                    EC(nc.psum_tensor(_u(f"lnp_ps_q{i}"), [1, LNW], F32)),
                    EC(nc.sbuf_tensor(_u(f"lnp_bc{i}"), [128, 2 * LNW], F32)),
                ))
            w0 = EC(nc.sbuf_tensor(_u("a1k_w0"), [128, NCH * 128], BF16))
            w1 = EC(nc.sbuf_tensor(_u("a1k_w1"), [128, NCH * 128], BF16))
            ob0 = EC(nc.sbuf_tensor(_u("a1k_ob0"), [128, 512], BF16))
            ob1 = EC(nc.sbuf_tensor(_u("a1k_ob1"), [128, 512], BF16))
            ps0 = EC(nc.psum_tensor(_u("a1k_ps0"), [128, 512], F32))
            ps1 = EC(nc.psum_tensor(_u("a1k_ps1"), [128, 512], F32))
            wsb = (w0, w1); obs = (ob0, ob1); pss = (ps0, ps1)
            s = Sched()
            it = [0]
            loaded = set()

            def load_w(sl):
                if sl in loaded:
                    return
                loaded.add(sl)
                s.add("gpsimd", lambda e, sl=sl: e.dma_start(
                    wsb[sl % 2][:].rearrange("p (c j) -> p c j", j=128),
                    Wk.rearrange("(c p) j -> p c j", p=128)
                    [:, :, sl * 128:(sl + 1) * 128]),
                    outs=[("kw", sl % 2)], dma=dG)

            def emit_k(sl, tt):
                load_w(sl)
                tkeys = [("xlnP", 2 * tt + i) for i in range(2)]
                h = sl
                pi = it[0] % 2

                def t_mm(e, sl=sl, tt=tt, pi=pi):
                    ins = None
                    for c in range(NCH):
                        ins = e.matmul(
                            pss[pi][:],
                            wsb[sl % 2][:, c * 128:(c + 1) * 128],
                            xlnP[:, c * SEQ + tt * 512:
                                 c * SEQ + tt * 512 + 512],
                            start=(c == 0), stop=(c == NCH - 1))
                    return ins
                s.add("tensor", t_mm, ins=[("kw", sl % 2)] + tkeys,
                      outs=[("ps", pi)])

                s.add("scalar", lambda e, h=h, pi=pi: e.activation(
                    obs[pi][:], pss[pi][:], AF.Identity,
                    bias=bk_cols[:, h:h + 1]),
                    ins=[("ps", pi), "bk"], outs=[("ob", pi)])

                s.add("sync", lambda e, h=h, tt=tt, pi=pi: e.dma_start(
                    k_scr[:, h * SEQ + tt * 512:h * SEQ + tt * 512 + 512],
                    obs[pi][:]), ins=[("ob", pi)], outs=["k_scr"], dma=dS)
                it[0] += 1

            load_w(0)
            for j in range(NLT):
                su = j % 2
                xt = xtile2[su]
                xbf, tmp, rows, ps_s, ps_q, bcab = lnb[su]
                s.add("sync" if su == 0 else "gpsimd",
                      lambda e, j=j, xt=xt: e.dma_start(
                          xt[:], xT[j * 128:(j + 1) * 128, :]),
                      outs=[("ln_xtile2", su)],
                      dma=dS if su == 0 else dG)
                emit_ln_tile(
                    s,
                    src_of_c=lambda c, xt=xt: xt[:, c * LNW:(c + 1) * LNW],
                    dst_key=("xlnP", j),
                    dst_of_c=lambda c, j=j: xlnP[:, c * SEQ + j * LNW:
                                                 c * SEQ + j * LNW + LNW],
                    bufs=(xbf, tmp, rows, ps_s[:1, :], ps_q[:1, :], bcab),
                    src_keys=[("ln_xtile2", su)], W=LNW, su=su)
                if j % 2 == 1:
                    emit_k(0, j // 2)
            for sl in range(1, 16):
                for tt in range(8):
                    emit_k(sl, tt)
            s.add("sync", lambda e: None, extra=[(dS, dS.n)])
            s.flush()

    def emit_proj_v(xln_sb, xln_keys):
        with (
            nc.sbuf_tensor(_u("a1v_w"), [128, NCH * 1024], BF16) as wv_sb,
            nc.sbuf_tensor(_u("a1v_bvr"), [1, D], F32) as bv_row,
            nc.sbuf_tensor(_u("a1v_bvbc"), [128, 1024], F32) as bv_bc,
            nc.sbuf_tensor(_u("a1v_ob0"), [128, 1024], BF16) as ob0,
            nc.sbuf_tensor(_u("a1v_ob1"), [128, 1024], BF16) as ob1,
            nc.psum_tensor(_u("a1v_ps0"), [128, 1024], F32) as ps0,
            nc.psum_tensor(_u("a1v_ps1"), [128, 1024], F32) as ps1,
            nc.psum_tensor(_u("a1v_psbv"), [128, 1024], F32) as psbv,
        ):
            obs = (ob0, ob1); pss = (ps0, ps1)
            s = Sched()
            s.add("sync", lambda e: e.dma_start(
                bv_row[:], bv.rearrange("(o d) -> o d", o=1)), outs=["bvr"], dma=dS)
            for hg in range(2):
                s.add("gpsimd", lambda e, hg=hg: e.dma_start(
                    wv_sb[:].rearrange("p (c j) -> p c j", j=1024),
                    Wv.rearrange("(c p) j -> p c j", p=128)
                    [:, :, hg * 1024:(hg + 1) * 1024]), outs=["wv"], dma=dG)

                def t_bv(e, hg=hg):
                    e.matmul(psbv[:, 0:512], ones_row[:],
                             bv_row[:1, hg * 1024:hg * 1024 + 512],
                             start=True, stop=True)
                    return e.matmul(psbv[:, 512:1024], ones_row[:],
                                    bv_row[:1, hg * 1024 + 512:(hg + 1) * 1024],
                                    start=True, stop=True)
                s.add("tensor", t_bv, ins=["bvr", "ones"], outs=["psbv"])
                s.add("vector", lambda e: e.tensor_copy(bv_bc[:], psbv[:]),
                      ins=["psbv"], outs=["bvbc"])

                for tb in range(32):
                    pi = tb % 2

                    def t_mm(e, tb=tb, pi=pi):
                        ins = None
                        for c in range(NCH):
                            ins = _mm2(
                                e, pss[pi][:],
                                xln_sb[:, c * SEQ + tb * 128:c * SEQ + tb * 128 + 128],
                                wv_sb[:, c * 1024:(c + 1) * 1024],
                                start=(c == 0), stop=(c == NCH - 1))
                        return ins
                    s.add("tensor", t_mm, ins=["wv"] + list(xln_keys),
                          outs=[("vps", pi)])

                    s.add("vector", lambda e, pi=pi: e.tensor_tensor(
                        out=obs[pi][:], in0=pss[pi][:], in1=bv_bc[:], op=ALU.add),
                        ins=[("vps", pi), "bvbc"], outs=[("vob", pi)])

                    s.add("sync", lambda e, tb=tb, hg=hg, pi=pi: e.dma_start(
                        v_scr[:, tb * 2048 + hg * 1024:tb * 2048 + hg * 1024 + 1024],
                        obs[pi][:]), ins=[("vob", pi)], outs=["v_scr"], dma=dS)
            s.add("sync", lambda e: None, extra=[(dS, dS.n)])
            s.flush()

    def emit_attn(attnout_sb):
        with contextlib.ExitStack() as _st:
            EC = _st.enter_context
            masks = EC(nc.sbuf_tensor(_u("a2_masks"), [128, 32 * QT], BF16))
            kT = EC(nc.sbuf_tensor(_u("a2_kT"), [128, GH * SEQ], BF16))
            qTt = EC(nc.sbuf_tensor(_u("a2_qT"), [128, GH * TOWN], BF16))
            vsb = EC(nc.sbuf_tensor(_u("a2_v"), [128, 32 * GH * 128], BF16))
            exb = [[EC(nc.sbuf_tensor(_u(f"a2_ex{st}{j}"), [128, 1024], BF16))
                    for j in range(3)] for st in range(2)]
            red = [EC(nc.sbuf_tensor(_u(f"a2_red{st}"), [128, QT], BF16))
                   for st in range(2)]
            stk = [EC(nc.sbuf_tensor(_u(f"a2_stk{st}"), [128, 2 * QT], BF16))
                   for st in range(2)]
            rrow2 = EC(nc.sbuf_tensor(_u("a2_rrow"), [33, QT], F32))
            rrow = [rrow2[0:1, :], rrow2[32:33, :]]
            avout = [EC(nc.sbuf_tensor(_u(f"a2_avo{st}"), [128, QT], BF16))
                     for st in range(2)]
            bcsb = [EC(nc.sbuf_tensor(_u(f"a2_bcsb{st}"), [128, QT], F32))
                    for st in range(2)]
            scb = [EC(nc.psum_tensor(_u(f"a2_sc{st}"), [128, 1024], F32))
                   for st in range(2)]
            avb = [EC(nc.psum_tensor(_u(f"a2_av{st}"), [128, QT], F32))
                   for st in range(2)]
            pssum = [EC(nc.psum_tensor(_u(f"a2_sum{st}"), [1, QT], F32))
                     for st in range(2)]
            s = Sched()

            def v_masks(e):
                for site in range(32):
                    e.tensor_scalar(
                        out=masks[:, site * QT:(site + 1) * QT], in0=iota_sw[:],
                        scalar1=nd_bc[:, site:site + 1], scalar2=None,
                        op0=ALU.is_ge)
                e.drain()
                ins = None
                for site in range(32):
                    ins = e.tensor_scalar(
                        out=masks[:, site * QT:(site + 1) * QT],
                        in0=masks[:, site * QT:(site + 1) * QT],
                        scalar1=1.0, scalar2=10000.0,
                        op0=ALU.subtract, op1=ALU.mult)
                return ins
            s.add("vector", v_masks, ins=["iota", "ndbc"], outs=["masks"])

            pend_t2 = [None, None]

            def emit_pair(g, st, hl, qi, u, npair):
                masked = u >= npair - 4
                pi = u % 3

                def t_sc(e, hl=hl, qi=qi, u=u, st=st, masked=masked,
                         npair=npair):
                    ins = None
                    for i in range(2):
                        kb = 2 * u + i
                        ins = e.matmul(
                            scb[st][:, i * QT:(i + 1) * QT],
                            kT[:, hl * SEQ + kb * 128:hl * SEQ + kb * 128 + 128],
                            qTt[:, hl * TOWN + qi * QT:hl * TOWN + (qi + 1) * QT],
                            start=True, stop=not masked)
                        if masked:
                            site = qi * 8 + (u - (npair - 4)) * 2 + i
                            ins = e.matmul(
                                scb[st][:, i * QT:(i + 1) * QT],
                                ident_bf[:],
                                masks[:, site * QT:(site + 1) * QT],
                                start=False, stop=True)
                    return ins
                s.add("tensor", t_sc,
                      ins=["a2k", "a2q"] + (["masks", "ident"] if masked
                                            else []),
                      outs=[("sc", st)])

                s.add("scalar", lambda e, st=st, pi=pi: e.activation(
                    exb[st][pi][:], scb[st][:], AF.Exp, scale=SCALE),
                    ins=[("sc", st)], outs=[("ex", st, pi)])

                if u % 2 == 0:
                    def v_red(e, st=st, pi=pi):
                        return e.tensor_tensor(
                            out=red[st][:], in0=exb[st][pi][:, 0:QT],
                            in1=exb[st][pi][:, QT:2 * QT], op=ALU.add)
                    s.add("vector", v_red, ins=[("ex", st, pi)],
                          outs=[("red", st)])
                else:
                    rg = (u // 2) % 2

                    def v_s1(e, st=st, pi=pi, rg=rg):
                        return e.tensor_tensor(
                            out=stk[st][:, rg * QT:(rg + 1) * QT],
                            in0=red[st][:], in1=exb[st][pi][:, 0:QT],
                            op=ALU.add)
                    s.add("vector", v_s1, ins=[("ex", st, pi), ("red", st)],
                          outs=[("stkp", st, rg), ("stk", st, rg)])

                    def v_s2(e, st=st, pi=pi, rg=rg):
                        return e.tensor_tensor(
                            out=stk[st][:, rg * QT:(rg + 1) * QT],
                            in0=stk[st][:, rg * QT:(rg + 1) * QT],
                            in1=exb[st][pi][:, QT:2 * QT], op=ALU.add)
                    s.add("vector", v_s2, ins=[("ex", st, pi), ("stkp", st, rg)],
                          outs=[("stk", st, rg)])
                if u == 3 and pend_t2[st] is not None:
                    pend_t2[st]()
                    pend_t2[st] = None
                if u >= 3 and u % 2 == 1:
                    emit_sum(st, (u - 3) // 2, first=(u == 3), last=False)
                if u >= 2:
                    emit_av(st, hl, qi, u - 2, npair)

            def emit_sum(st, j, first, last):
                rg = j % 2

                def t_sum(e, st=st, rg=rg, first=first, last=last):
                    return e.matmul(pssum[st][:1, :], ones_bf[:],
                                    stk[st][:, rg * QT:(rg + 1) * QT],
                                    start=first, stop=last)
                s.add("tensor", t_sum, ins=[("stk", st, rg), "ones"],
                      outs=[("sum", st)])

            def emit_av(st, hl, qi, u, npair):
                def t_av(e, st=st, hl=hl, qi=qi, u=u, npair=npair):
                    ins = None
                    for i in range(2):
                        kb = 2 * u + i
                        ins = e.matmul(
                            avb[st][:],
                            vsb[:, kb * 512 + hl * 128:kb * 512 + hl * 128 + 128],
                            exb[st][u % 3][:, i * QT:(i + 1) * QT],
                            start=(u == 0 and i == 0),
                            stop=(u == npair - 1 and i == 1))
                    return ins
                s.add("tensor", t_av, ins=[("ex", st, u % 3), "a2v"],
                      outs=[("av", st)])

            def emit_tile_end(g, st, hl, qi, npair):
                emit_av(st, hl, qi, npair - 2, npair)
                emit_av(st, hl, qi, npair - 1, npair)
                emit_sum(st, npair // 2 - 1, first=(npair == 2), last=True)

                h = g * GH + hl

                def v_ev(e, st=st):
                    return e.tensor_copy(avout[st][:], avb[st][:])
                s.add("vector", v_ev, ins=[("av", st)], outs=[("avo", st)])

                def v_t1(e, st=st):
                    return e.reciprocal(rrow[st], pssum[st][:1, :])
                s.add("vector", v_t1, ins=[("sum", st)], outs=[("rrow", st)])
                s.add("sync", lambda e, st=st: e.dma_start(
                    nrm_scr[st:st + 1, :], rrow[st]),
                    ins=[("rrow", st)], outs=[("nscr", st)], dma=dS)

                def d_bc(e, st=st):
                    src = nrm_scr[st:st + 1, :]
                    bsrc = bass.AP(src.tensor, src.offset,
                                   [[0, 128]] + list(src.ap[1:]))
                    return e.dma_start(bcsb[st][:], bsrc)
                s.add("sync", d_bc, ins=[("nscr", st)], outs=[("bcsb", st)],
                      dma=dS)

                def emit_t2(st=st, h=h, qi=qi):
                    def v_t2(e, st=st, h=h, qi=qi):
                        return e.tensor_tensor(
                            out=attnout_sb[:, h * TOWN + qi * QT:
                                           h * TOWN + (qi + 1) * QT],
                            in0=avout[st][:], in1=bcsb[st][:], op=ALU.mult)
                    s.add("vector", v_t2, ins=[("bcsb", st), ("avo", st)],
                          outs=["attnout"])
                pend_t2[st] = emit_t2

            for g in range(NG):
                s.add("sync", lambda e, g=g: e.dma_start(
                    kT[:], k_scr[:, g * GH * SEQ:(g + 1) * GH * SEQ]),
                    ins=["k_scr"], outs=["a2k"], dma=dS)
                s.add("sync", lambda e, g=g: e.dma_start(
                    qTt[:], q_scr[:, g * GH * TOWN:(g + 1) * GH * TOWN]),
                    ins=["q_scr"], outs=["a2q"], dma=dS)
                for tb in range(32):
                    s.add("sync", lambda e, g=g, tb=tb: e.dma_start(
                        vsb[:, tb * 512:(tb + 1) * 512],
                        v_scr[:, tb * 2048 + g * 512:tb * 2048 + g * 512 + 512]),
                        ins=["v_scr"], outs=["a2v"], dma=dS)

                def events(st):
                    ev = []
                    for hl in (2 * st, 2 * st + 1):
                        for qi in range(NQT):
                            npair = KBMAX[qi] // 2
                            for u in range(npair):
                                ev.append(("pair", st, hl, qi, u, npair))
                            ev.append(("end", st, hl, qi, npair))
                    return ev
                evA, evB = events(0), events(1)
                assert len(evA) == len(evB)
                for ea, eb in zip(evA, evB):
                    for kind, st, hl, qi, x, *rest in (ea, eb):
                        if kind == "pair":
                            emit_pair(g, st, hl, qi, x, rest[0])
                        else:
                            emit_tile_end(g, st, hl, qi, x)
            for st in range(2):
                if pend_t2[st] is not None:
                    pend_t2[st]()
                    pend_t2[st] = None
            s.flush()

    def emit_wo(attnout_sb):
        with (
            nc.sbuf_tensor(_u("a3_wo"), [128, NH * D], BF16) as wo_sb,
            nc.sbuf_tensor(_u("a3_xt"), [128, NCH * QT], F32) as xt,
            nc.sbuf_tensor(_u("a3_ob0"), [128, QT], F32) as ob0,
            nc.sbuf_tensor(_u("a3_ob1"), [128, QT], F32) as ob1,
            nc.sbuf_tensor(_u("a3_x20"), [128, QT], F32) as x20,
            nc.sbuf_tensor(_u("a3_x21"), [128, QT], F32) as x21,
            nc.psum_tensor(_u("a3_ps0"), [128, QT], F32) as ps0,
            nc.psum_tensor(_u("a3_ps1"), [128, QT], F32) as ps1,
        ):
            obs = (ob0, ob1); x2s = (x20, x21); pss = (ps0, ps1)
            s = Sched()
            for j in range(16):
                s.add("gpsimd", lambda e, j=j: e.dma_start(
                    wo_sb[:, j * 2048:(j + 1) * 2048]
                    .rearrange("p (h jj) -> p h jj", jj=128),
                    Wo.rearrange("(h p) jj -> p h jj", p=128)
                    [:, :, j * 128:(j + 1) * 128]), outs=["wo"], dma=dG)
            it = 0
            for qt4 in range(4):
                s.add("sync", lambda e, qt4=qt4: e.dma_start(
                    xt[:].rearrange("p (c t) -> p c t", t=QT),
                    xTown.rearrange("(c p) t -> p c t", p=128)
                    [:, :, qt4 * QT:(qt4 + 1) * QT]), outs=["a3xt"], dma=dS)
                for j in range(16):
                    pi = it % 2

                    def t_mm(e, j=j, qt4=qt4, pi=pi):
                        ins = None
                        for h in range(NH):
                            ins = e.matmul(
                                pss[pi][:],
                                wo_sb[:, j * 2048 + h * 128:j * 2048 + h * 128 + 128],
                                attnout_sb[:, h * TOWN + qt4 * QT:
                                           h * TOWN + (qt4 + 1) * QT],
                                start=(h == 0), stop=(h == NH - 1))
                        return ins
                    s.add("tensor", t_mm, ins=["wo", "attnout"], outs=[("wops", pi)])

                    s.add("scalar", lambda e, j=j, pi=pi: e.activation(
                        obs[pi][:], pss[pi][:], AF.Identity, bias=bo_cols[:, j:j + 1]),
                        ins=[("wops", pi), "bo"], outs=[("wob", pi)])

                    s.add("vector", lambda e, j=j, pi=pi: e.tensor_tensor(
                        out=x2s[pi][:], in0=obs[pi][:],
                        in1=xt[:, j * QT:(j + 1) * QT], op=ALU.add),
                        ins=[("wob", pi), "a3xt"], outs=[("x2b", pi)])

                    s.add("sync", lambda e, j=j, qt4=qt4, pi=pi: e.dma_start(
                        x2_scr[:, j * TOWN + qt4 * QT:j * TOWN + qt4 * QT + QT],
                        x2s[pi][:]), ins=[("x2b", pi)], outs=["x2_scr"], dma=dS)
                    it += 1
            s.add("sync", lambda e: None, extra=[(dS, dS.n)])
            s.flush()

    def emit_ffn():
        with (
            nc.sbuf_tensor(_u("b_xln"), [128, NCH * FT], BF16) as xln2,
            nc.sbuf_tensor(_u("b_abc"), [128, 8 * FT], BF16) as abc,
            nc.sbuf_tensor(_u("b_wacc"), [128, 8 * FT], F32) as wacc,
            nc.sbuf_tensor(_u("b_smid"), [128, NCH * FT], BF16) as smid,
        ):
            for tt in range(NFT):
                with contextlib.ExitStack() as _st:
                    EC = _st.enter_context
                    x2t2 = [EC(nc.sbuf_tensor(_u("bl_x2t"),
                                              [128, NCH * QT], F32))] * 2
                    xbf = EC(nc.sbuf_tensor(_u("bl_xbf"), [128, NCH * QT],
                                            BF16))
                    tmp = EC(nc.sbuf_tensor(_u("bl_tmp"), [128, QT], F32))
                    rows = EC(nc.sbuf_tensor(_u("bl_rows"), [1, 5 * QT], F32))
                    ps_s = EC(nc.psum_tensor(_u("bl_ps_s"), [1, QT], F32))
                    ps_q = EC(nc.psum_tensor(_u("bl_ps_q"), [1, QT], F32))
                    bcab = EC(nc.sbuf_tensor(_u("bl_bc"), [128, 2 * QT], F32))
                    s = Sched()
                    for sub in range(2):
                        su = sub % 2
                        base = tt * FT + sub * QT
                        xt = x2t2[su]
                        s.add("sync" if su == 0 else "gpsimd",
                              lambda e, base=base, xt=xt: e.dma_start(
                                  xt[:].rearrange("p (c t) -> p c t", t=QT),
                                  x2_scr.rearrange("p (c t) -> p c t", c=NCH)
                                  [:, :, base:base + QT]),
                              ins=["x2_scr"], outs=[("bx2t", 0)],
                              dma=dS if su == 0 else dG)
                        emit_ln_tile(
                            s,
                            src_of_c=lambda c, xt=xt: xt[:, c * QT:
                                                         (c + 1) * QT],
                            dst_key="xln2" if sub == 1 else ("xln2s", sub),
                            dst_of_c=lambda c, sub=sub: xln2[
                                :, c * FT + sub * QT:c * FT + sub * QT + QT],
                            bufs=(xbf, tmp, rows, ps_s[:1, :], ps_q[:1, :],
                                  bcab),
                            src_keys=[("bx2t", 0)], W=QT, su=0)
                    s.flush()
                if tt == 0 and "xln2" in tap_outs:
                    s = Sched()
                    s.add("sync", lambda e: e.dma_start(tap_outs["xln2"], xln2[:]),
                          ins=["xln2"], dma=dS)
                    s.add("sync", lambda e: None, extra=[(dS, dS.n)])
                    s.flush()

                with (
                    nc.sbuf_tensor(_u("br_wr"), [128, NCH * 128], BF16) as wr_sb,
                    nc.sbuf_tensor(_u("br_e"), [128, 128], F32) as e_sb,
                    nc.sbuf_tensor(_u("br_rows"), [128, 12], F32) as rws,
                    nc.sbuf_tensor(_u("br_mbf"), [128, 128], BF16) as mbf,
                    nc.sbuf_tensor(_u("br_mT"), [128, 8 * 128], BF16) as mT,
                    nc.psum_tensor(_u("br_psr"), [128, 128], F32) as psr,
                    nc.psum_tensor(_u("br_psT"), [128, 128], BF16) as psT,
                    nc.psum_tensor(_u("br_psal"), [128, QT], F32) as psal,
                    nc.psum_tensor(_u("br_psbc"), [128, QT], F32) as psbc2,
                ):
                    s = Sched()
                    s.add("gpsimd", lambda e: e.dma_start(
                        wr_sb[:].rearrange("p (c j) -> p c j", j=128),
                        w_router.rearrange("(c p) j -> p c j", p=128)),
                        outs=["wr"], dma=dG)
                    for sub in range(8):
                        def t_r(e, sub=sub):
                            ins = None
                            for c in range(NCH):
                                ins = e.matmul(
                                    psr[:],
                                    xln2[:, c * FT + sub * 128:c * FT + sub * 128 + 128],
                                    wr_sb[:, c * 128:(c + 1) * 128],
                                    start=(c == 0), stop=(c == NCH - 1))
                            return ins
                        s.add("tensor", t_r, ins=["wr", "xln2"], outs=["psr"])

                        def v_max(e):
                            e.reduce_max(rws[:, 0:1], psr[:], axis=AX.X)
                            e.drain()
                            e.tensor_scalar(out=rws[:, 1:2], in0=rws[:, 0:1],
                                            scalar1=-1.0, scalar2=None, op0=ALU.mult)
                            return e.drain()
                        s.add("vector", v_max, ins=["psr"], outs=["nmax"])

                        s.add("scalar", lambda e: e.activation(
                            e_sb[:], psr[:], AF.Exp, bias=rws[:, 1:2],
                            accum_out=rws[:, 2:3]),
                            ins=["psr", "nmax"], outs=["esb", "zrow"])

                        def v_top4(e):
                            e.max(out=rws[:, 4:12], in_=e_sb[:])
                            e.drain()
                            e.tensor_scalar(out=mbf[:], in0=e_sb[:],
                                            scalar1=rws[:, 7:8], scalar2=None,
                                            op0=ALU.is_ge)
                            e.tensor_tensor(out=e_sb[:], in0=e_sb[:], in1=mbf[:],
                                            op=ALU.mult)
                            e.drain()
                            e.reduce_sum(rws[:, 3:4], e_sb[:], axis=AX.X)
                            e.tensor_scalar(out=rws[:, 4:5], in0=rws[:, 2:3],
                                            scalar1=float(EPS_R), scalar2=None,
                                            op0=ALU.mult)
                            e.drain()
                            e.tensor_tensor(out=rws[:, 3:4], in0=rws[:, 3:4],
                                            in1=rws[:, 4:5], op=ALU.add)
                            e.drain()
                            e.reciprocal(rws[:, 3:4], rws[:, 3:4])
                            e.drain()
                            e.tensor_scalar(out=e_sb[:], in0=e_sb[:],
                                            scalar1=rws[:, 3:4], scalar2=None,
                                            op0=ALU.mult)
                            return e.tensor_tensor(out=mbf[:], in0=e_sb[:],
                                                   in1=esc_bc[:], op=ALU.mult)
                        s.add("vector", v_top4, ins=["esb", "zrow", "escbc"],
                              outs=["mbf"])

                        s.add("tensor", lambda e: e.transpose(psT[:], mbf[:],
                                                              ident_bf[:]),
                              ins=["mbf", "ident"], outs=["psT"])
                        s.add("vector", lambda e, sub=sub: e.tensor_copy(
                            mT[:, sub * 128:(sub + 1) * 128], psT[:]),
                            ins=["psT"], outs=["mT"])

                    alps = (psal, psbc2)
                    for kh in range(16):
                        k, half = kh // 2, kh % 2
                        pa = kh % 2
                        s.add("tensor", lambda e, k=k, half=half, pa=pa: e.matmul(
                            alps[pa][:], mixM[:, k * 128:(k + 1) * 128],
                            mT[:, half * QT:(half + 1) * QT],
                            start=True, stop=True),
                            ins=["mT", "mix"], outs=[("psal", pa)])
                        s.add("vector", lambda e, k=k, half=half, pa=pa:
                              e.tensor_copy(
                                  abc[:, k * FT + half * QT:
                                      k * FT + half * QT + QT],
                                  alps[pa][:]), ins=[("psal", pa)], outs=["abc"])
                    if tt == 0 and "mT" in tap_outs:
                        s.add("sync", lambda e: e.dma_start(tap_outs["mT"], mT[:]),
                              ins=["mT"], dma=dS)
                        s.add("sync", lambda e: e.dma_start(tap_outs["abc"], abc[:]),
                              ins=["abc"], dma=dS)
                        s.add("sync", lambda e: None, extra=[(dS, dS.n)])
                    s.flush()

                with (
                    nc.sbuf_tensor(_u("b2_wg0"), [128, NCH * 128], BF16) as wg0,
                    nc.sbuf_tensor(_u("b2_wg1"), [128, NCH * 128], BF16) as wg1,
                    nc.sbuf_tensor(_u("b2_wu0"), [128, NCH * 128], BF16) as wu0,
                    nc.sbuf_tensor(_u("b2_wu1"), [128, NCH * 128], BF16) as wu1,
                    nc.sbuf_tensor(_u("b2_sil0"), [128, FT], F32) as sil0,
                    nc.sbuf_tensor(_u("b2_sil1"), [128, FT], F32) as sil1,
                    nc.sbuf_tensor(_u("b2_bas0"), [128, FT], BF16) as bas0,
                    nc.sbuf_tensor(_u("b2_bas1"), [128, FT], BF16) as bas1,
                    nc.sbuf_tensor(_u("b2_bs0"), [128, FT], BF16) as bs0,
                    nc.sbuf_tensor(_u("b2_bs1"), [128, FT], BF16) as bs1,
                    nc.psum_tensor(_u("b2_psA"), [128, FT], F32) as psA,
                    nc.psum_tensor(_u("b2_psB"), [128, FT], F32) as psB,
                    nc.psum_tensor(_u("b2_psC"), [128, FT], F32) as psC,
                ):
                    wgs = (wg0, wg1); wus = (wu0, wu1); sils = (sil0, sil1)
                    bas = (bas0, bas1); bss = (bs0, bs1); gps = (psA, psB)
                    s = Sched()
                    for kic in range(64):
                        k, ic = kic // 8, kic % 8
                        pi = kic % 2
                        s.add("gpsimd", lambda e, kic=kic, pi=pi: e.dma_start(
                            wgs[pi][:].rearrange("p (c j) -> p c j", j=128),
                            w_gate.rearrange("(c p) j -> p c j", p=128)
                            [:, :, kic * 128:(kic + 1) * 128]),
                            outs=[("wg", pi)], dma=dG)
                        s.add("gpsimd", lambda e, kic=kic, pi=pi: e.dma_start(
                            wus[pi][:].rearrange("p (c j) -> p c j", j=128),
                            w_up.rearrange("(c p) j -> p c j", p=128)
                            [:, :, kic * 128:(kic + 1) * 128]),
                            outs=[("wu", pi)], dma=dG)

                        def t_g(e, pi=pi):
                            ins = None
                            for c in range(NCH):
                                ins = _mm2(
                                    e, gps[pi][:], wgs[pi][:, c * 128:(c + 1) * 128],
                                    xln2[:, c * FT:(c + 1) * FT],
                                    start=(c == 0), stop=(c == NCH - 1))
                            return ins
                        s.add("tensor", t_g, ins=[("wg", pi), "xln2"],
                              outs=[("gps", pi)])

                        def t_u(e, pi=pi):
                            ins = None
                            for c in range(NCH):
                                ins = _mm2(
                                    e, psC[:], wus[pi][:, c * 128:(c + 1) * 128],
                                    xln2[:, c * FT:(c + 1) * FT],
                                    start=(c == 0), stop=(c == NCH - 1))
                            return ins
                        s.add("tensor", t_u, ins=[("wu", pi), "xln2"], outs=["psC"])

                        s.add("scalar", lambda e, pi=pi: e.activation(
                            sils[pi][:], gps[pi][:], AF.Silu),
                            ins=[("gps", pi)], outs=[("sil", pi)])

                        s.add("vector", lambda e, pi=pi: e.tensor_tensor(
                            out=bas[pi][:], in0=sils[pi][:], in1=psC[:], op=ALU.mult),
                            ins=[("sil", pi), "psC"], outs=[("bas", pi)])

                        s.add("vector", lambda e, k=k, pi=pi: e.tensor_tensor(
                            out=bss[pi][:], in0=bas[pi][:],
                            in1=abc[:, k * FT:(k + 1) * FT], op=ALU.mult),
                            ins=[("bas", pi), "abc"], outs=[("bs", pi)])

                        if k == 0:
                            s.add("vector", lambda e, ic=ic, pi=pi: e.tensor_copy(
                                wacc[:, ic * FT:(ic + 1) * FT], bss[pi][:]),
                                ins=[("bs", pi)], outs=[("wacc", ic)])
                        else:
                            def v_acc(e, ic=ic, pi=pi):
                                return e.tensor_tensor(
                                    out=wacc[:, ic * FT:(ic + 1) * FT],
                                    in0=wacc[:, ic * FT:(ic + 1) * FT],
                                    in1=bss[pi][:], op=ALU.add)
                            s.add("vector", v_acc, ins=[("bs", pi), ("wacc", ic)],
                                  outs=[("wacc", ic)])
                    if tt == 0 and "wacc" in tap_outs:
                        s.add("sync", lambda e: e.dma_start(tap_outs["wacc"], wacc[:]),
                              ins=[("wacc", ic) for ic in range(8)], dma=dS)
                        s.add("sync", lambda e: None, extra=[(dS, dS.n)])
                    s.flush()

                with (
                    nc.sbuf_tensor(_u("b3_wg0"), [128, NCH * 128], BF16) as swg0,
                    nc.sbuf_tensor(_u("b3_wg1"), [128, NCH * 128], BF16) as swg1,
                    nc.sbuf_tensor(_u("b3_wu0"), [128, NCH * 128], BF16) as swu0,
                    nc.sbuf_tensor(_u("b3_wu1"), [128, NCH * 128], BF16) as swu1,
                    nc.sbuf_tensor(_u("b3_sil0"), [128, FT], F32) as ssil0,
                    nc.sbuf_tensor(_u("b3_sil1"), [128, FT], F32) as ssil1,
                    nc.psum_tensor(_u("b3_psA"), [128, FT], F32) as psA3,
                    nc.psum_tensor(_u("b3_psB"), [128, FT], F32) as psB3,
                    nc.psum_tensor(_u("b3_psC"), [128, FT], F32) as psC3,
                ):
                    wgs = (swg0, swg1); wus = (swu0, swu1); sils = (ssil0, ssil1)
                    gps = (psA3, psB3)
                    s = Sched()
                    for oc in range(16):
                        pi = oc % 2
                        s.add("gpsimd", lambda e, oc=oc, pi=pi: e.dma_start(
                            wgs[pi][:].rearrange("p (c j) -> p c j", j=128),
                            swg.rearrange("(c p) j -> p c j", p=128)
                            [:, :, oc * 128:(oc + 1) * 128]),
                            outs=[("swg", pi)], dma=dG)
                        s.add("gpsimd", lambda e, oc=oc, pi=pi: e.dma_start(
                            wus[pi][:].rearrange("p (c j) -> p c j", j=128),
                            swu.rearrange("(c p) j -> p c j", p=128)
                            [:, :, oc * 128:(oc + 1) * 128]),
                            outs=[("swu", pi)], dma=dG)

                        def t_g(e, pi=pi):
                            ins = None
                            for c in range(NCH):
                                ins = _mm2(
                                    e, gps[pi][:], wgs[pi][:, c * 128:(c + 1) * 128],
                                    xln2[:, c * FT:(c + 1) * FT],
                                    start=(c == 0), stop=(c == NCH - 1))
                            return ins
                        s.add("tensor", t_g, ins=[("swg", pi), "xln2"],
                              outs=[("sgps", pi)])

                        def t_u(e, pi=pi):
                            ins = None
                            for c in range(NCH):
                                ins = _mm2(
                                    e, psC3[:], wus[pi][:, c * 128:(c + 1) * 128],
                                    xln2[:, c * FT:(c + 1) * FT],
                                    start=(c == 0), stop=(c == NCH - 1))
                            return ins
                        s.add("tensor", t_u, ins=[("swu", pi), "xln2"], outs=["psC3"])

                        s.add("scalar", lambda e, pi=pi: e.activation(
                            sils[pi][:], gps[pi][:], AF.Silu),
                            ins=[("sgps", pi)], outs=[("ssil", pi)])

                        s.add("vector", lambda e, oc=oc, pi=pi: e.tensor_tensor(
                            out=smid[:, oc * FT:(oc + 1) * FT], in0=sils[pi][:],
                            in1=psC3[:], op=ALU.mult),
                            ins=[("ssil", pi), "psC3"], outs=["smid"])
                    if tt == 0 and "smid" in tap_outs:
                        s.add("sync", lambda e: e.dma_start(tap_outs["smid"], smid[:]),
                              ins=["smid"], dma=dS)
                        s.add("sync", lambda e: None, extra=[(dS, dS.n)])
                    s.flush()

                with (
                    nc.sbuf_tensor(_u("b4_wbf"), [128, 8 * FT], BF16) as wbf,
                    nc.sbuf_tensor(_u("b4_wd0"), [128, 8 * 128], BF16) as wd0,
                    nc.sbuf_tensor(_u("b4_wd1"), [128, 8 * 128], BF16) as wd1,
                    nc.sbuf_tensor(_u("b4_wsd0"), [128, NCH * 128], BF16) as wsd0,
                    nc.sbuf_tensor(_u("b4_wsd1"), [128, NCH * 128], BF16) as wsd1,
                    nc.sbuf_tensor(_u("b4_x20"), [128, FT], F32) as x20,
                    nc.sbuf_tensor(_u("b4_x21"), [128, FT], F32) as x21,
                    nc.sbuf_tensor(_u("b4_yb0"), [128, FT], F32) as yb0,
                    nc.sbuf_tensor(_u("b4_yb1"), [128, FT], F32) as yb1,
                    nc.psum_tensor(_u("b4_ps0"), [128, FT], F32) as bps0,
                    nc.psum_tensor(_u("b4_ps1"), [128, FT], F32) as bps1,
                ):
                    wds = (wd0, wd1); wsds = (wsd0, wsd1)
                    x2s = (x20, x21); ybs = (yb0, yb1); pss = (bps0, bps1)
                    s = Sched()

                    def v_cast(e):
                        return e.tensor_copy(wbf[:], wacc[:])
                    s.add("vector", v_cast,
                          ins=[("wacc", ic) for ic in range(8)], outs=["wbf"])

                    for j in range(16):
                        pi = j % 2
                        s.add("gpsimd", lambda e, j=j, pi=pi: e.dma_start(
                            wds[pi][:].rearrange("p (c jj) -> p c jj", jj=128),
                            w_down.rearrange("(c p) jj -> p c jj", p=128)
                            [:, :, j * 128:(j + 1) * 128]),
                            outs=[("wd", pi)], dma=dG)
                        s.add("gpsimd", lambda e, j=j, pi=pi: e.dma_start(
                            wsds[pi][:].rearrange("p (c jj) -> p c jj", jj=128),
                            swd.rearrange("(c p) jj -> p c jj", p=128)
                            [:, :, j * 128:(j + 1) * 128]),
                            outs=[("wsd", pi)], dma=dG)
                        s.add("sync", lambda e, j=j, tt=tt, pi=pi: e.dma_start(
                            x2s[pi][:],
                            x2_scr[:, j * TOWN + tt * FT:j * TOWN + tt * FT + FT]),
                            ins=["x2_scr"], outs=[("bx2", pi)], dma=dS)

                        def t_mm(e, j=j, pi=pi):
                            ins = None
                            for ic in range(8):
                                _mm2(e, pss[pi][:],
                                     wds[pi][:, ic * 128:(ic + 1) * 128],
                                     wbf[:, ic * FT:(ic + 1) * FT],
                                     start=(ic == 0), stop=False)
                            for oc in range(NCH):
                                ins = _mm2(e, pss[pi][:],
                                           wsds[pi][:, oc * 128:(oc + 1) * 128],
                                           smid[:, oc * FT:(oc + 1) * FT],
                                           start=False, stop=(oc == NCH - 1))
                            return ins
                        s.add("tensor", t_mm,
                              ins=[("wd", pi), ("wsd", pi), "wbf", "smid"],
                              outs=[("bps", pi)])

                        s.add("vector", lambda e, pi=pi: e.tensor_tensor(
                            out=ybs[pi][:], in0=pss[pi][:], in1=x2s[pi][:],
                            op=ALU.add),
                            ins=[("bps", pi), ("bx2", pi)], outs=[("yb", pi)])

                        s.add("sync", lambda e, j=j, tt=tt, pi=pi: e.dma_start(
                            yT[j * 128:(j + 1) * 128, tt * FT:(tt + 1) * FT],
                            ybs[pi][:]), ins=[("yb", pi)], outs=["yT"], dma=dS)
                    s.add("sync", lambda e: None, extra=[(dS, dS.n)])
                    s.flush()

    def tap(name, src):
        if name in tap_outs:
            s = Sched()
            s.add("sync", lambda e: e.dma_start(tap_outs[name], src), dma=dS)
            s.add("sync", lambda e: None, extra=[(dS, dS.n)])
            s.flush()

    if "A0b" in phases or "A1q" in phases:
        emit_phase_A0bq()
    if "A0" in phases or "A1k" in phases or "A1v" in phases:
        with nc.sbuf_tensor(_u("xlnP"), [128, NCH * SEQ], BF16) as xlnP:
            emit_phase_A0k(xlnP)
            if "A1v" in phases:
                emit_proj_v(xlnP, [("xlnP", j) for j in range(NLT)])
    tap("q_scr", q_scr)
    tap("k_scr", k_scr)
    tap("v_scr", v_scr)
    if "A2" in phases or "A3" in phases:
        with nc.sbuf_tensor(_u("attnout"), [128, NH * TOWN], BF16) as attnout_sb:
            if "A2" in phases:
                emit_attn(attnout_sb)
                tap("attnout", attnout_sb[:])
            if "A3" in phases:
                emit_wo(attnout_sb)
    tap("x2", x2_scr)
    if "B" in phases:
        emit_ffn()

    s = Sched()
    s.add("sync", lambda e: None, extra=[(dS, dS.n), (dG, dG.n)])
    s.add("gpsimd", lambda e: None, extra=[(dG, dG.n)])
    s.flush()
    return nc


def _scram_cols(par):
    own = OWN_BLOCKS[par]
    jj = np.arange(QT)
    off = (jj % 32) * 16 + jj // 32
    return np.concatenate([blk * QT + off for blk in own])


def _qsrc_cols(par):
    own = OWN_BLOCKS[par]
    cols = []
    for H in range(16):
        for blk in own:
            cols.append(H * 256 + blk * 32 + np.arange(32))
    return np.concatenate(cols)


def _core_inputs(inputs, core):
    b = core // 2
    par = core % 2
    x = inputs["L0_attn_x"]
    xT_b = np.ascontiguousarray(x[b].T)
    own = OWN_BLOCKS[par]
    cols = _scram_cols(par)
    xTown = np.ascontiguousarray(xT_b[:, cols])
    xQown_flat = xT_b[:, _qsrc_cols(par)]
    xT_tiled = np.ascontiguousarray(
        x[b].reshape(16, 256, 16, 128).transpose(0, 3, 2, 1)
        .reshape(16 * 128, 16 * 256))
    xQown_tiled = np.ascontiguousarray(
        xQown_flat.T.reshape(8, 256, 16, 128).transpose(0, 3, 2, 1)
        .reshape(8 * 128, 16 * 256))
    nd = np.zeros((1, 32), np.float32)
    for qi in range(NQT):
        B = own[qi] * QT
        for r in range(8):
            kb = KBMAX[qi] - 8 + r
            nd[0, qi * 8 + r] = 128 * kb - B
    return {
        "xT": xT_tiled, "xTown": xTown, "xQown": xQown_tiled, "negdelta": nd,
        "Wq": inputs["L0_attn_Wq"], "Wk": inputs["L0_attn_Wk"],
        "Wv": inputs["L0_attn_Wv"], "Wo": inputs["L0_attn_Wo"],
        "bq": inputs["L0_attn_bq"], "bk": inputs["L0_attn_bk"],
        "bv": inputs["L0_attn_bv"], "bo": inputs["L0_attn_bo"],
        "w_router": inputs["L0_ffn_w_router"],
        "w_gate": inputs["L0_ffn_w_gate"], "w_up": inputs["L0_ffn_w_up"],
        "w_down": inputs["L0_ffn_w_down"],
        "amp2": np.ascontiguousarray(
            inputs["L0_ffn_amplitudes"].reshape(128, 16)),
        "esc": np.ascontiguousarray(
            inputs["L0_ffn_expert_scale"].reshape(1, 128)),
        "swg": inputs["L0_ffn_shared_w_gate"],
        "swu": inputs["L0_ffn_shared_w_up"],
        "swd": inputs["L0_ffn_shared_w_down"],
    }


def kernel(**inputs):
    nc = build_program()
    in_maps = [
        {k: np.ascontiguousarray(np.asarray(v, dtype=np.float32))
         for k, v in _core_inputs(inputs, c).items()}
        for c in range(8)
    ]
    res = run_bass_kernel_spmd(nc, in_maps, core_ids=list(range(8)))
    out = np.empty((4, SEQ, D), np.float32)
    for c in range(8):
        b = c // 2
        cols = _scram_cols(c % 2)
        yT_c = res.results[c]["yT"]
        out[b, cols, :] = yT_c.T
    return out


# version 19
# speedup vs baseline: 1.0732x; 1.0189x over previous
import contextlib

import numpy as np
import concourse.bass as bass
import concourse.mybir as mybir
from concourse.bass_utils import run_bass_kernel_spmd

F32 = mybir.dt.float32
BF16 = mybir.dt.bfloat16
I32 = mybir.dt.int32
AF = mybir.ActivationFunctionType
ALU = mybir.AluOpType
AX = mybir.AxisListType

D = 2048
NCH = 16
SEQ = 4096
TOWN = 2048
QT = 512
NQT = 4
KBMAX = (8, 16, 24, 32)
NH = 16
DH = 128
GH = 4
NG = 4
SCALE = float(1.0 / np.sqrt(128.0))
EPS_LN = 1e-5
EPS_R = 1e-8
FT = 1024
NFT = 2
OWN_BLOCKS = ((0, 3, 4, 7), (1, 2, 5, 6))
ENGS = ("sync", "tensor", "scalar", "vector", "gpsimd")


def _mm2(e, out_ap, lhsT, rhs, start, stop):
    n = rhs.shape[-1]
    ins = None
    for off in range(0, n, 512):
        w = min(512, n - off)
        ins = e.matmul(out_ap[:, off:off + w], lhsT, rhs[:, off:off + w],
                       start=start, stop=stop)
    return ins


ALL_PHASES = ("W0", "A0b", "A1q", "A0", "A1k", "A1v", "A2", "A3", "B")


def build_program(phases=ALL_PHASES, taps=(), ffn_taps=False):
    nc = bass.Bass("TRN2", target_bir_lowering=False, debug=False)

    def din(name, shape):
        return nc.dram_tensor(name, shape, F32, kind="ExternalInput").ap()

    xT = din("xT", [16 * 128, 16 * 256])
    xTown = din("xTown", [D, TOWN])
    xQown = din("xQown", [8 * 128, 16 * 256])
    negdelta = din("negdelta", [1, 32])
    Wq = din("Wq", [D, D]); Wk = din("Wk", [D, D]); Wv = din("Wv", [D, D])
    Wo = din("Wo", [D, D])
    bq = din("bq", [D]); bk = din("bk", [D]); bv = din("bv", [D]); bo = din("bo", [D])
    w_router = din("w_router", [D, 128])
    w_gate = din("w_gate", [D, 8192]); w_up = din("w_up", [D, 8192])
    w_down = din("w_down", [1024, D])
    amp2 = din("amp2", [128, 16]); esc = din("esc", [1, 128])
    swg = din("swg", [D, D]); swu = din("swu", [D, D]); swd = din("swd", [D, D])
    yT = nc.dram_tensor("yT", [D, TOWN], F32, kind="ExternalOutput").ap()

    q_scr = nc.dram_tensor("q_scr", [128, NH * TOWN], BF16, kind="Internal").ap()
    k_scr = nc.dram_tensor("k_scr", [128, NH * SEQ], BF16, kind="Internal").ap()
    v_scr = nc.dram_tensor("v_scr", [128, 32 * D], BF16, kind="Internal").ap()
    x2_scr = nc.dram_tensor("x2_scr", [128, NCH * TOWN], F32, kind="Internal").ap()
    nrm_scr = nc.dram_tensor("nrm_scr", [2, QT], F32, kind="Internal").ap()
    ln_scr = nc.dram_tensor("ln_scr", [2, 2 * QT], F32, kind="Internal").ap()

    tap_outs = {}
    if ffn_taps:
        for t, shape, dt in (("xln2", [128, NCH * FT], BF16),
                             ("mT", [128, 8 * 128], BF16),
                             ("abc", [128, 8 * FT], BF16),
                             ("wacc", [128, 8 * FT], F32),
                             ("smid", [128, NCH * FT], BF16)):
            tap_outs[t] = nc.dram_tensor("tap_" + t, shape, dt,
                                         kind="ExternalOutput").ap()
    for t in taps:
        shape, dt = {
            "xlnO": ([128, NCH * TOWN], BF16), "xlnP": ([128, NCH * SEQ], BF16),
            "q_scr": ([128, NH * TOWN], BF16), "k_scr": ([128, NH * SEQ], BF16),
            "v_scr": ([128, 32 * D], BF16), "attnout": ([128, NH * TOWN], BF16),
            "x2": ([128, NCH * TOWN], F32)}[t]
        tap_outs[t] = nc.dram_tensor("tap_" + t, shape, dt, kind="ExternalOutput").ap()


    _uid_ctr = [0]
    def _u(name):
        _uid_ctr[0] += 1
        return f"{name}_{_uid_ctr[0]}"

    class Ct:
        __slots__ = ("h", "n", "eng")
        def __init__(self, name, eng):
            self.h = nc.alloc_semaphore(name=name)
            self.n = 0
            self.eng = eng

    dS = Ct("dS", None)
    dG = Ct("dG", None)
    dA = Ct("dA", None)
    cts = {"tensor": Ct("sPE", "tensor"), "scalar": Ct("sSC", "scalar"),
           "vector": Ct("sDV", "vector"), "gpsimd": Ct("sPO", "gpsimd"),
           "sync": Ct("sSY", "sync")}

    ready = {}
    free = {}

    class Sched:
        def __init__(self):
            self.ev = {e: [] for e in ENGS}

        def add(self, eng, fn, ins=(), outs=(), extra=(), dma=None):
            waits = {}

            def addw(w):
                prev = waits.get(id(w[0]))
                waits[id(w[0])] = (w[0], max(w[1], prev[1] if prev else 0))

            for k in ins:
                w = ready.get(k)
                if w:
                    addw(w)
            for k in outs:
                for w in free.get(k, ()):
                    addw(w)
                w = ready.get(k)
                if w:
                    addw(w)
            for w in extra:
                addw(w)
            if dma is not None:
                ct, delta = dma, 16
            else:
                ct, delta = cts[eng], 1
            ct.n += delta
            self.ev[eng].append((fn, [(c, v) for c, v in waits.values()], ct, delta))
            mark = (ct, ct.n)
            for k in outs:
                ready[k] = mark
                free[k] = []
            for k in ins:
                free.setdefault(k, []).append(mark)

        def flush(self):
            with nc.Block() as blk:
                for eng in ENGS:
                    evs = self.ev[eng]
                    if not evs:
                        continue

                    def body(e, evs=evs):
                        for fn, waits, ct, delta in evs:
                            for c, v in waits:
                                e.wait_ge(c.h, v)
                            ins = fn(e)
                            if ins is None:
                                ins = e.nop()
                            ins.then_inc(ct.h, delta)

                    getattr(blk, eng)(body)
            self.ev = {e: [] for e in ENGS}

    ones_bf = nc.alloc_sbuf_tensor("c_ones_bf", [128, 1], BF16)
    ones_row = nc.alloc_sbuf_tensor("c_ones_row", [1, 128], F32)
    one128 = nc.alloc_sbuf_tensor("c_one128", [128, 128], BF16)
    iota_sw = nc.alloc_sbuf_tensor("c_iota_sw", [128, QT], F32)
    nd_bc = nc.alloc_sbuf_tensor("c_nd_bc", [128, 32], F32)
    ident_bf = nc.alloc_sbuf_tensor("c_ident", [128, 128], BF16)
    bq_cols = nc.alloc_sbuf_tensor("c_bq", [128, 16], F32)
    bk_cols = nc.alloc_sbuf_tensor("c_bk", [128, 16], F32)
    bo_cols = nc.alloc_sbuf_tensor("c_bo", [128, 16], F32)
    esc_bc = nc.alloc_sbuf_tensor("c_esc_bc", [128, 128], F32)
    mixM = nc.alloc_sbuf_tensor("c_mixM", [128, 8 * 128], BF16)

    if "W0" in phases:
        with (
            nc.sbuf_tensor(_u("w0_ioti"), [128, QT], I32) as ioti,
            nc.sbuf_tensor(_u("w0_amp"), [128, 16], F32) as amp_sb,
            nc.sbuf_tensor(_u("w0_p8"), [128, 8], F32) as p8,
            nc.sbuf_tensor(_u("w0_r8"), [128, 1], F32) as r8,
            nc.sbuf_tensor(_u("w0_esc_row"), [1, 128], F32) as esc_row,
            nc.sbuf_tensor(_u("w0_nd_row"), [1, 32], F32) as nd_row,
            nc.psum_tensor(_u("w0_psa"), [128, 128], F32) as psa,
            nc.psum_tensor(_u("w0_psb"), [128, 32], F32) as psb,
        ):
            s = Sched()
            s.add("sync", lambda e: e.dma_start(amp_sb[:], amp2), outs=["amp"], dma=dS)
            s.add("sync", lambda e: e.dma_start(esc_row[:], esc), outs=["escr"], dma=dS)
            s.add("sync", lambda e: e.dma_start(nd_row[:], negdelta), outs=["ndr"], dma=dS)
            for c in range(16):
                for nm, col, src in (("bq", bq_cols, bq), ("bk", bk_cols, bk),
                                     ("bo", bo_cols, bo)):
                    s.add("sync", lambda e, col=col, src=src, c=c: e.dma_start(
                        col[:, c:c + 1],
                        src[c * 128:(c + 1) * 128].rearrange("(p o) -> p o", o=1)),
                        outs=[nm], dma=dS)

            def g_iota(e):
                e.iota(ioti[:], [[1, QT]], base=0, channel_multiplier=-1)
                e.drain()
                e.tensor_scalar(out=ident_bf[:], in0=ioti[:, 0:128],
                                scalar1=0, scalar2=None, op0=ALU.is_equal)
                e.drain()
                e.iota(ioti[:].rearrange("p (w t) -> p w t", t=32),
                       [[1, 16], [16, 32]], base=0, channel_multiplier=-1)
                e.drain()
                e.tensor_copy(iota_sw[:], ioti[:])
                e.memset(ones_bf[:], 1.0)
                e.memset(ones_row[:], 1.0)
                e.memset(one128[:], 1.0)
                return e.drain()
            s.add("gpsimd", g_iota, outs=["iota", "ones", "ident"])

            s.add("scalar", lambda e: e.activation(amp_sb[:], amp_sb[:], AF.Square),
                  ins=["amp"], outs=["amp2q"])
            s.add("tensor", lambda e: e.matmul(psa[:], ones_row[:], esc_row[:],
                                               start=True, stop=True),
                  ins=["escr", "ones"], outs=["psa0"])
            s.add("tensor", lambda e: e.matmul(psb[:], ones_row[:], nd_row[:],
                                               start=True, stop=True),
                  ins=["ndr", "ones"], outs=["psb0"])

            def v_mix(e):
                e.tensor_copy(esc_bc[:], psa[:])
                e.tensor_copy(nd_bc[:], psb[:])
                e.tensor_tensor(out=p8[:], in0=amp_sb[:, 0:16:2],
                                in1=amp_sb[:, 1:16:2], op=ALU.add)
                e.drain()
                e.reduce_sum(r8[:], p8[:], axis=AX.X)
                e.drain()
                e.tensor_scalar(out=r8[:], in0=r8[:], scalar1=float(EPS_R),
                                scalar2=None, op0=ALU.add)
                e.drain()
                e.reciprocal(r8[:], r8[:])
                e.drain()
                e.tensor_scalar(out=p8[:], in0=p8[:], scalar1=r8[:, 0:1],
                                scalar2=None, op0=ALU.mult)
                e.drain()
                ins = None
                for k in range(8):
                    ins = e.tensor_scalar(
                        out=mixM[:, k * 128:(k + 1) * 128], in0=one128[:],
                        scalar1=p8[:, k:k + 1], scalar2=None, op0=ALU.mult)
                return ins
            s.add("vector", v_mix, ins=["psa0", "psb0", "amp2q", "ones"],
                  outs=["escbc", "ndbc", "mix"])
            s.flush()

    def emit_ln_tile(s, src_wide, dst_wide3, dst_key, bufs, src_keys, W=QT,
                     su=0):
        xbf, rows, ps_s, ps_q, bca, bcb = bufs

        def K(n):
            return (n, su)

        def RR(i):
            return rows[:1, i * W:(i + 1) * W]

        s.add("vector", lambda e: e.tensor_copy(xbf[:], src_wide),
              ins=list(src_keys), outs=[K("ln_xbf")])

        def t_sum(e):
            ins = None
            for c in range(NCH):
                ins = e.matmul(ps_s, ones_bf[:], xbf[:, c * W:(c + 1) * W],
                               start=(c == 0), stop=(c == NCH - 1))
            return ins
        s.add("tensor", t_sum, ins=[K("ln_xbf"), "ones"], outs=[K("ln_pss")])

        s.add("scalar", lambda e: e.activation(xbf[:], src_wide, AF.Square),
              ins=list(src_keys) + [K("ln_pss"), K("ln_xbf")],
              outs=[K("ln_xbf")])

        def t_sq(e):
            ins = None
            for c in range(NCH):
                ins = e.matmul(ps_q, ones_bf[:], xbf[:, c * W:(c + 1) * W],
                               start=(c == 0), stop=(c == NCH - 1))
            return ins
        s.add("tensor", t_sq, ins=[K("ln_xbf")], outs=[K("ln_psq")])

        def v_stats(e):
            e.tensor_scalar(out=RR(0), in0=ps_s, scalar1=float(1.0 / D),
                            scalar2=None, op0=ALU.mult)
            e.tensor_scalar(out=RR(1), in0=ps_q, scalar1=float(1.0 / D),
                            scalar2=None, op0=ALU.mult)
            return e.drain()
        s.add("vector", v_stats, ins=[K("ln_pss"), K("ln_psq")],
              outs=[K("ln_r01")])

        s.add("scalar", lambda e: e.activation(RR(2), RR(0), AF.Square),
              ins=[K("ln_r01")], outs=[K("ln_r2")])

        def v_var(e):
            e.tensor_tensor(out=RR(1), in0=RR(1), in1=RR(2), op=ALU.subtract)
            return e.drain()
        s.add("vector", v_var, ins=[K("ln_r2"), K("ln_r01")], outs=[K("ln_var")])

        s.add("scalar", lambda e: e.activation(RR(2), RR(1), AF.Sqrt,
                                               bias=np.float32(EPS_LN)),
              ins=[K("ln_var"), K("ln_r2")], outs=[K("ln_std")])

        def v_ab(e):
            e.reciprocal(RR(3), RR(2))
            e.drain()
            e.scalar_tensor_tensor(out=RR(4), in0=RR(0), scalar=-1.0, in1=RR(3),
                                   op0=ALU.mult, op1=ALU.mult)
            return e.drain()
        s.add("vector", v_ab, ins=[K("ln_std"), K("ln_r01")], outs=[K("ln_ab")])

        s.add("scalar", lambda e: e.dma_start(
            ln_scr[su:su + 1, 0:2 * W], rows[:1, 3 * W:5 * W]),
            ins=[K("ln_ab")], outs=[K("ln_scr")], dma=dA)

        def d_bca(e):
            srcp = ln_scr[su:su + 1, 0:W]
            bsrc = bass.AP(srcp.tensor, srcp.offset,
                           [[0, 128]] + list(srcp.ap[1:]))
            return e.dma_start(bca[:], bsrc)
        s.add("scalar", d_bca, ins=[K("ln_scr")], outs=[K("ln_bca")], dma=dA)

        def d_bcb(e):
            srcp = ln_scr[su:su + 1, W:2 * W]
            bsrc = bass.AP(srcp.tensor, srcp.offset,
                           [[0, 128]] + list(srcp.ap[1:]))
            return e.dma_start(bcb[:], bsrc)
        s.add("gpsimd", d_bcb, ins=[K("ln_scr")], outs=[K("ln_bcb")], dma=dG)

        def bc3(t):
            ap = t[:, :]
            return bass.AP(ap.tensor, ap.offset, [ap.ap[0], [0, NCH], ap.ap[1]])

        def v_n1(e):
            return e.tensor_tensor(
                out=xbf[:].rearrange("p (c t) -> p c t", t=W),
                in0=src_wide.rearrange("p (c t) -> p c t", t=W),
                in1=bc3(bca), op=ALU.mult)
        s.add("vector", v_n1, ins=[K("ln_bca"), K("ln_xbf")] + list(src_keys),
              outs=[K("ln_xbf")])

        def v_n2(e):
            return e.tensor_tensor(
                out=dst_wide3, in0=xbf[:].rearrange("p (c t) -> p c t", t=W),
                in1=bc3(bcb), op=ALU.add)
        s.add("vector", v_n2, ins=[K("ln_bcb"), K("ln_xbf")], outs=[dst_key])

    def emit_phase_A0bq():
        LW = 256
        with contextlib.ExitStack() as _st:
            EC = _st.enter_context
            xlnO = EC(nc.sbuf_tensor(_u("xlnO"), [128, NCH * TOWN], BF16))
            xtile2 = [EC(nc.sbuf_tensor(_u(f"lnq_xt{i}"), [128, NCH * LW],
                                        F32)) for i in range(2)]
            lnb = []
            for i in range(2):
                lnb.append((
                    EC(nc.sbuf_tensor(_u(f"lnq_xbf{i}"), [128, NCH * LW],
                                      BF16)),
                    EC(nc.sbuf_tensor(_u(f"lnq_rows{i}"), [1, 5 * LW], F32)),
                    EC(nc.psum_tensor(_u(f"lnq_ps_s{i}"), [1, LW], F32)),
                    EC(nc.psum_tensor(_u(f"lnq_ps_q{i}"), [1, LW], F32)),
                    EC(nc.sbuf_tensor(_u(f"lnq_bca{i}"), [128, LW], F32)),
                    EC(nc.sbuf_tensor(_u(f"lnq_bcb{i}"), [128, LW], BF16)),
                ))
            wqb = [EC(nc.sbuf_tensor(_u(f"a1qb_w{i}"), [128, NCH * 128], BF16))
                   for i in range(6)]
            qstage = EC(nc.sbuf_tensor(_u("a1qb_qs"), [128, 4 * 2048], BF16))
            ps0 = EC(nc.psum_tensor(_u("a1qb_ps0"), [128, QT], F32))
            ps1 = EC(nc.psum_tensor(_u("a1qb_ps1"), [128, QT], F32))
            pss = (ps0, ps1)
            s = Sched()

            def emit_lno(j):
                su = j % 2
                xt = xtile2[su]
                xbf, rows, ps_s, ps_q, bca, bcb = lnb[su]
                s.add("sync" if su == 0 else "gpsimd",
                      lambda e, j=j, xt=xt: e.dma_start(
                          xt[:], xQown[j * 128:(j + 1) * 128, :]),
                      outs=[("ln_xtile", su)],
                      dma=dS if su == 0 else dG)
                emit_ln_tile(
                    s, src_wide=xt[:],
                    dst_wide3=xlnO[:].rearrange("p (c t) -> p c t", t=TOWN)
                    [:, :, j * LW:(j + 1) * LW],
                    dst_key=("xlnO", j),
                    bufs=(xbf, rows, ps_s[:1, :], ps_q[:1, :], bca, bcb),
                    src_keys=[("ln_xtile", su)], W=LW, su=su)

            it = 0
            emit_lno(0)
            for hg in range(4):
                emit_lno(2 * hg + 1)
                if hg < 3:
                    emit_lno(2 * hg + 2)
                for w in range(16):
                    wk = ("qw", w % 6)
                    s.add("gpsimd", lambda e, w=w: e.dma_start(
                        wqb[w % 6][:].rearrange("p (c j) -> p c j", j=128),
                        Wq.rearrange("(c p) j -> p c j", p=128)
                        [:, :, w * 128:(w + 1) * 128]), outs=[wk], dma=dG)
                    pi = it % 2

                    def t_mm(e, hg=hg, w=w, pi=pi):
                        ins = None
                        for c in range(NCH):
                            ins = e.matmul(
                                pss[pi][:],
                                wqb[w % 6][:, c * 128:(c + 1) * 128],
                                xlnO[:, c * TOWN + hg * 512:
                                     c * TOWN + hg * 512 + 512],
                                start=(c == 0), stop=(c == NCH - 1))
                        return ins
                    s.add("tensor", t_mm,
                          ins=[wk, ("xlnO", 2 * hg), ("xlnO", 2 * hg + 1)],
                          outs=[("qps", pi)])

                    def sc_ev(e, w=w, pi=pi):
                        ins = None
                        for hl in range(4):
                            src_ap = pss[pi][:, hl * 128:(hl + 1) * 128] \
                                .rearrange("p (s t) -> p s t", t=32)
                            dst = qstage[:, hl * 2048:(hl + 1) * 2048] \
                                .rearrange("p (s q) -> p s q", q=512) \
                                [:, :, w * 32:w * 32 + 32]
                            ins = e.activation(dst, src_ap, AF.Identity,
                                               bias=bq_cols[:, w:w + 1])
                        return ins
                    s.add("scalar", sc_ev, ins=[("qps", pi), "bq"],
                          outs=[("qstage", w)])
                    it += 1
                for hl in range(4):
                    H = hg * 4 + hl
                    s.add("sync", lambda e, H=H, hl=hl: e.dma_start(
                        q_scr[:, H * TOWN:(H + 1) * TOWN],
                        qstage[:, hl * 2048:(hl + 1) * 2048]),
                        ins=[("qstage", w_) for w_ in range(16)],
                        outs=["q_scr"], dma=dS)
            s.add("sync", lambda e: None, extra=[(dS, dS.n)])
            s.flush()

    LNW = 256
    NLT = SEQ // LNW

    def emit_phase_A0k(xlnP):
        with contextlib.ExitStack() as _st:
            EC = _st.enter_context
            xtile2 = [EC(nc.sbuf_tensor(_u(f"lnp_xt{i}"), [128, NCH * LNW],
                                        F32)) for i in range(2)]
            lnb = []
            for i in range(2):
                lnb.append((
                    EC(nc.sbuf_tensor(_u(f"lnp_xbf{i}"), [128, NCH * LNW],
                                      BF16)),
                    EC(nc.sbuf_tensor(_u(f"lnp_rows{i}"), [1, 5 * LNW], F32)),
                    EC(nc.psum_tensor(_u(f"lnp_ps_s{i}"), [1, LNW], F32)),
                    EC(nc.psum_tensor(_u(f"lnp_ps_q{i}"), [1, LNW], F32)),
                    EC(nc.sbuf_tensor(_u(f"lnp_bca{i}"), [128, LNW], F32)),
                    EC(nc.sbuf_tensor(_u(f"lnp_bcb{i}"), [128, LNW], BF16)),
                ))
            w0 = EC(nc.sbuf_tensor(_u("a1k_w0"), [128, NCH * 128], BF16))
            w1 = EC(nc.sbuf_tensor(_u("a1k_w1"), [128, NCH * 128], BF16))
            ob0 = EC(nc.sbuf_tensor(_u("a1k_ob0"), [128, 512], BF16))
            ob1 = EC(nc.sbuf_tensor(_u("a1k_ob1"), [128, 512], BF16))
            ps0 = EC(nc.psum_tensor(_u("a1k_ps0"), [128, 512], F32))
            ps1 = EC(nc.psum_tensor(_u("a1k_ps1"), [128, 512], F32))
            wsb = (w0, w1); obs = (ob0, ob1); pss = (ps0, ps1)
            s = Sched()
            it = [0]
            loaded = set()

            def load_w(sl):
                if sl in loaded:
                    return
                loaded.add(sl)
                s.add("gpsimd", lambda e, sl=sl: e.dma_start(
                    wsb[sl % 2][:].rearrange("p (c j) -> p c j", j=128),
                    Wk.rearrange("(c p) j -> p c j", p=128)
                    [:, :, sl * 128:(sl + 1) * 128]),
                    outs=[("kw", sl % 2)], dma=dG)

            def emit_k(sl, tt):
                load_w(sl)
                tkeys = [("xlnP", 2 * tt + i) for i in range(2)]
                h = sl
                pi = it[0] % 2

                def t_mm(e, sl=sl, tt=tt, pi=pi):
                    ins = None
                    for c in range(NCH):
                        ins = e.matmul(
                            pss[pi][:],
                            wsb[sl % 2][:, c * 128:(c + 1) * 128],
                            xlnP[:, c * SEQ + tt * 512:
                                 c * SEQ + tt * 512 + 512],
                            start=(c == 0), stop=(c == NCH - 1))
                    return ins
                s.add("tensor", t_mm, ins=[("kw", sl % 2)] + tkeys,
                      outs=[("ps", pi)])

                s.add("scalar", lambda e, h=h, pi=pi: e.activation(
                    obs[pi][:], pss[pi][:], AF.Identity,
                    bias=bk_cols[:, h:h + 1]),
                    ins=[("ps", pi), "bk"], outs=[("ob", pi)])

                s.add("sync", lambda e, h=h, tt=tt, pi=pi: e.dma_start(
                    k_scr[:, h * SEQ + tt * 512:h * SEQ + tt * 512 + 512],
                    obs[pi][:]), ins=[("ob", pi)], outs=["k_scr"], dma=dS)
                it[0] += 1

            load_w(0)
            for j in range(NLT):
                su = j % 2
                xt = xtile2[su]
                xbf, rows, ps_s, ps_q, bca, bcb = lnb[su]
                s.add("sync" if su == 0 else "gpsimd",
                      lambda e, j=j, xt=xt: e.dma_start(
                          xt[:], xT[j * 128:(j + 1) * 128, :]),
                      outs=[("ln_xtile2", su)],
                      dma=dS if su == 0 else dG)
                emit_ln_tile(
                    s, src_wide=xt[:],
                    dst_wide3=xlnP[:].rearrange("p (c t) -> p c t", t=SEQ)
                    [:, :, j * LNW:(j + 1) * LNW],
                    dst_key=("xlnP", j),
                    bufs=(xbf, rows, ps_s[:1, :], ps_q[:1, :], bca, bcb),
                    src_keys=[("ln_xtile2", su)], W=LNW, su=su)
                if j % 2 == 1:
                    emit_k(0, j // 2)
            for sl in range(1, 16):
                for tt in range(8):
                    emit_k(sl, tt)
            s.add("sync", lambda e: None, extra=[(dS, dS.n)])
            s.flush()

    def emit_proj_v(xln_sb, xln_keys):
        with (
            nc.sbuf_tensor(_u("a1v_w"), [128, NCH * 1024], BF16) as wv_sb,
            nc.sbuf_tensor(_u("a1v_bvr"), [1, D], F32) as bv_row,
            nc.sbuf_tensor(_u("a1v_bvbc"), [128, 1024], F32) as bv_bc,
            nc.sbuf_tensor(_u("a1v_ob0"), [128, 1024], BF16) as ob0,
            nc.sbuf_tensor(_u("a1v_ob1"), [128, 1024], BF16) as ob1,
            nc.psum_tensor(_u("a1v_ps0"), [128, 1024], F32) as ps0,
            nc.psum_tensor(_u("a1v_ps1"), [128, 1024], F32) as ps1,
            nc.psum_tensor(_u("a1v_psbv"), [128, 1024], F32) as psbv,
        ):
            obs = (ob0, ob1); pss = (ps0, ps1)
            s = Sched()
            s.add("sync", lambda e: e.dma_start(
                bv_row[:], bv.rearrange("(o d) -> o d", o=1)), outs=["bvr"], dma=dS)
            for hg in range(2):
                s.add("gpsimd", lambda e, hg=hg: e.dma_start(
                    wv_sb[:].rearrange("p (c j) -> p c j", j=1024),
                    Wv.rearrange("(c p) j -> p c j", p=128)
                    [:, :, hg * 1024:(hg + 1) * 1024]), outs=["wv"], dma=dG)

                def t_bv(e, hg=hg):
                    e.matmul(psbv[:, 0:512], ones_row[:],
                             bv_row[:1, hg * 1024:hg * 1024 + 512],
                             start=True, stop=True)
                    return e.matmul(psbv[:, 512:1024], ones_row[:],
                                    bv_row[:1, hg * 1024 + 512:(hg + 1) * 1024],
                                    start=True, stop=True)
                s.add("tensor", t_bv, ins=["bvr", "ones"], outs=["psbv"])
                s.add("vector", lambda e: e.tensor_copy(bv_bc[:], psbv[:]),
                      ins=["psbv"], outs=["bvbc"])

                for tb in range(32):
                    pi = tb % 2

                    def t_mm(e, tb=tb, pi=pi):
                        ins = None
                        for c in range(NCH):
                            ins = _mm2(
                                e, pss[pi][:],
                                xln_sb[:, c * SEQ + tb * 128:c * SEQ + tb * 128 + 128],
                                wv_sb[:, c * 1024:(c + 1) * 1024],
                                start=(c == 0), stop=(c == NCH - 1))
                        return ins
                    s.add("tensor", t_mm, ins=["wv"] + list(xln_keys),
                          outs=[("vps", pi)])

                    s.add("vector", lambda e, pi=pi: e.tensor_tensor(
                        out=obs[pi][:], in0=pss[pi][:], in1=bv_bc[:], op=ALU.add),
                        ins=[("vps", pi), "bvbc"], outs=[("vob", pi)])

                    s.add("sync", lambda e, tb=tb, hg=hg, pi=pi: e.dma_start(
                        v_scr[:, tb * 2048 + hg * 1024:tb * 2048 + hg * 1024 + 1024],
                        obs[pi][:]), ins=[("vob", pi)], outs=["v_scr"], dma=dS)
            s.add("sync", lambda e: None, extra=[(dS, dS.n)])
            s.flush()

    def emit_attn(attnout_sb):
        with contextlib.ExitStack() as _st:
            EC = _st.enter_context
            masks = EC(nc.sbuf_tensor(_u("a2_masks"), [128, 32 * QT], BF16))
            kT = EC(nc.sbuf_tensor(_u("a2_kT"), [128, GH * SEQ], BF16))
            qTt = EC(nc.sbuf_tensor(_u("a2_qT"), [128, GH * TOWN], BF16))
            vsb = EC(nc.sbuf_tensor(_u("a2_v"), [128, 32 * GH * 128], BF16))
            exb = [[EC(nc.sbuf_tensor(_u(f"a2_ex{st}{j}"), [128, 1024], BF16))
                    for j in range(3)] for st in range(2)]
            red = [EC(nc.sbuf_tensor(_u(f"a2_red{st}"), [128, QT], BF16))
                   for st in range(2)]
            stk = [EC(nc.sbuf_tensor(_u(f"a2_stk{st}"), [128, 2 * QT], BF16))
                   for st in range(2)]
            rrow2 = EC(nc.sbuf_tensor(_u("a2_rrow"), [33, QT], F32))
            rrow = [rrow2[0:1, :], rrow2[32:33, :]]
            avout = [EC(nc.sbuf_tensor(_u(f"a2_avo{st}"), [128, QT], BF16))
                     for st in range(2)]
            bcsb = [EC(nc.sbuf_tensor(_u(f"a2_bcsb{st}"), [128, QT], F32))
                    for st in range(2)]
            scb = [EC(nc.psum_tensor(_u(f"a2_sc{st}"), [128, 1024], F32))
                   for st in range(2)]
            avb = [EC(nc.psum_tensor(_u(f"a2_av{st}"), [128, QT], F32))
                   for st in range(2)]
            pssum = [EC(nc.psum_tensor(_u(f"a2_sum{st}"), [1, QT], F32))
                     for st in range(2)]
            s = Sched()

            def v_masks(e):
                for site in range(32):
                    e.tensor_scalar(
                        out=masks[:, site * QT:(site + 1) * QT], in0=iota_sw[:],
                        scalar1=nd_bc[:, site:site + 1], scalar2=None,
                        op0=ALU.is_ge)
                e.drain()
                ins = None
                for site in range(32):
                    ins = e.tensor_scalar(
                        out=masks[:, site * QT:(site + 1) * QT],
                        in0=masks[:, site * QT:(site + 1) * QT],
                        scalar1=1.0, scalar2=10000.0,
                        op0=ALU.subtract, op1=ALU.mult)
                return ins
            s.add("vector", v_masks, ins=["iota", "ndbc"], outs=["masks"])

            pend_t2 = [None, None]

            def emit_pair(g, st, hl, qi, u, npair):
                masked = u >= npair - 4
                pi = u % 3

                def t_sc(e, hl=hl, qi=qi, u=u, st=st, masked=masked,
                         npair=npair):
                    ins = None
                    for i in range(2):
                        kb = 2 * u + i
                        ins = e.matmul(
                            scb[st][:, i * QT:(i + 1) * QT],
                            kT[:, hl * SEQ + kb * 128:hl * SEQ + kb * 128 + 128],
                            qTt[:, hl * TOWN + qi * QT:hl * TOWN + (qi + 1) * QT],
                            start=True, stop=not masked)
                        if masked:
                            site = qi * 8 + (u - (npair - 4)) * 2 + i
                            ins = e.matmul(
                                scb[st][:, i * QT:(i + 1) * QT],
                                ident_bf[:],
                                masks[:, site * QT:(site + 1) * QT],
                                start=False, stop=True)
                    return ins
                s.add("tensor", t_sc,
                      ins=["a2k", "a2q"] + (["masks", "ident"] if masked
                                            else []),
                      outs=[("sc", st)])

                s.add("scalar", lambda e, st=st, pi=pi: e.activation(
                    exb[st][pi][:], scb[st][:], AF.Exp, scale=SCALE),
                    ins=[("sc", st)], outs=[("ex", st, pi)])

                if u % 2 == 0:
                    def v_red(e, st=st, pi=pi):
                        return e.tensor_tensor(
                            out=red[st][:], in0=exb[st][pi][:, 0:QT],
                            in1=exb[st][pi][:, QT:2 * QT], op=ALU.add)
                    s.add("vector", v_red, ins=[("ex", st, pi)],
                          outs=[("red", st)])
                else:
                    rg = (u // 2) % 2

                    def v_s1(e, st=st, pi=pi, rg=rg):
                        return e.tensor_tensor(
                            out=stk[st][:, rg * QT:(rg + 1) * QT],
                            in0=red[st][:], in1=exb[st][pi][:, 0:QT],
                            op=ALU.add)
                    s.add("vector", v_s1, ins=[("ex", st, pi), ("red", st)],
                          outs=[("stkp", st, rg), ("stk", st, rg)])

                    def v_s2(e, st=st, pi=pi, rg=rg):
                        return e.tensor_tensor(
                            out=stk[st][:, rg * QT:(rg + 1) * QT],
                            in0=stk[st][:, rg * QT:(rg + 1) * QT],
                            in1=exb[st][pi][:, QT:2 * QT], op=ALU.add)
                    s.add("vector", v_s2, ins=[("ex", st, pi), ("stkp", st, rg)],
                          outs=[("stk", st, rg)])
                if u == 3 and pend_t2[st] is not None:
                    pend_t2[st]()
                    pend_t2[st] = None
                if u >= 3 and u % 2 == 1:
                    emit_sum(st, (u - 3) // 2, first=(u == 3), last=False)
                if u >= 2:
                    emit_av(st, hl, qi, u - 2, npair)

            def emit_sum(st, j, first, last):
                rg = j % 2

                def t_sum(e, st=st, rg=rg, first=first, last=last):
                    return e.matmul(pssum[st][:1, :], ones_bf[:],
                                    stk[st][:, rg * QT:(rg + 1) * QT],
                                    start=first, stop=last)
                s.add("tensor", t_sum, ins=[("stk", st, rg), "ones"],
                      outs=[("sum", st)])

            def emit_av(st, hl, qi, u, npair):
                def t_av(e, st=st, hl=hl, qi=qi, u=u, npair=npair):
                    ins = None
                    for i in range(2):
                        kb = 2 * u + i
                        ins = e.matmul(
                            avb[st][:],
                            vsb[:, kb * 512 + hl * 128:kb * 512 + hl * 128 + 128],
                            exb[st][u % 3][:, i * QT:(i + 1) * QT],
                            start=(u == 0 and i == 0),
                            stop=(u == npair - 1 and i == 1))
                    return ins
                s.add("tensor", t_av, ins=[("ex", st, u % 3), "a2v"],
                      outs=[("av", st)])

            def emit_tile_end(g, st, hl, qi, npair):
                emit_av(st, hl, qi, npair - 2, npair)
                emit_av(st, hl, qi, npair - 1, npair)
                emit_sum(st, npair // 2 - 1, first=(npair == 2), last=True)

                h = g * GH + hl

                def v_ev(e, st=st):
                    return e.tensor_copy(avout[st][:], avb[st][:])
                s.add("vector", v_ev, ins=[("av", st)], outs=[("avo", st)])

                def v_t1(e, st=st):
                    return e.reciprocal(rrow[st], pssum[st][:1, :])
                s.add("vector", v_t1, ins=[("sum", st)], outs=[("rrow", st)])
                s.add("sync", lambda e, st=st: e.dma_start(
                    nrm_scr[st:st + 1, :], rrow[st]),
                    ins=[("rrow", st)], outs=[("nscr", st)], dma=dS)

                def d_bc(e, st=st):
                    src = nrm_scr[st:st + 1, :]
                    bsrc = bass.AP(src.tensor, src.offset,
                                   [[0, 128]] + list(src.ap[1:]))
                    return e.dma_start(bcsb[st][:], bsrc)
                s.add("sync", d_bc, ins=[("nscr", st)], outs=[("bcsb", st)],
                      dma=dS)

                def emit_t2(st=st, h=h, qi=qi):
                    def v_t2(e, st=st, h=h, qi=qi):
                        return e.tensor_tensor(
                            out=attnout_sb[:, h * TOWN + qi * QT:
                                           h * TOWN + (qi + 1) * QT],
                            in0=avout[st][:], in1=bcsb[st][:], op=ALU.mult)
                    s.add("vector", v_t2, ins=[("bcsb", st), ("avo", st)],
                          outs=["attnout"])
                pend_t2[st] = emit_t2

            for g in range(NG):
                s.add("sync", lambda e, g=g: e.dma_start(
                    kT[:], k_scr[:, g * GH * SEQ:(g + 1) * GH * SEQ]),
                    ins=["k_scr"], outs=["a2k"], dma=dS)
                s.add("sync", lambda e, g=g: e.dma_start(
                    qTt[:], q_scr[:, g * GH * TOWN:(g + 1) * GH * TOWN]),
                    ins=["q_scr"], outs=["a2q"], dma=dS)
                for tb in range(32):
                    s.add("sync", lambda e, g=g, tb=tb: e.dma_start(
                        vsb[:, tb * 512:(tb + 1) * 512],
                        v_scr[:, tb * 2048 + g * 512:tb * 2048 + g * 512 + 512]),
                        ins=["v_scr"], outs=["a2v"], dma=dS)

                def events(st):
                    ev = []
                    for hl in (2 * st, 2 * st + 1):
                        for qi in range(NQT):
                            npair = KBMAX[qi] // 2
                            for u in range(npair):
                                ev.append(("pair", st, hl, qi, u, npair))
                            ev.append(("end", st, hl, qi, npair))
                    return ev
                evA, evB = events(0), events(1)
                assert len(evA) == len(evB)
                for ea, eb in zip(evA, evB):
                    for kind, st, hl, qi, x, *rest in (ea, eb):
                        if kind == "pair":
                            emit_pair(g, st, hl, qi, x, rest[0])
                        else:
                            emit_tile_end(g, st, hl, qi, x)
            for st in range(2):
                if pend_t2[st] is not None:
                    pend_t2[st]()
                    pend_t2[st] = None
            s.flush()

    def emit_wo(attnout_sb):
        with (
            nc.sbuf_tensor(_u("a3_wo"), [128, NH * D], BF16) as wo_sb,
            nc.sbuf_tensor(_u("a3_xt"), [128, NCH * QT], F32) as xt,
            nc.sbuf_tensor(_u("a3_ob0"), [128, QT], F32) as ob0,
            nc.sbuf_tensor(_u("a3_ob1"), [128, QT], F32) as ob1,
            nc.sbuf_tensor(_u("a3_x20"), [128, QT], F32) as x20,
            nc.sbuf_tensor(_u("a3_x21"), [128, QT], F32) as x21,
            nc.psum_tensor(_u("a3_ps0"), [128, QT], F32) as ps0,
            nc.psum_tensor(_u("a3_ps1"), [128, QT], F32) as ps1,
        ):
            obs = (ob0, ob1); x2s = (x20, x21); pss = (ps0, ps1)
            s = Sched()
            for j in range(16):
                s.add("gpsimd", lambda e, j=j: e.dma_start(
                    wo_sb[:, j * 2048:(j + 1) * 2048]
                    .rearrange("p (h jj) -> p h jj", jj=128),
                    Wo.rearrange("(h p) jj -> p h jj", p=128)
                    [:, :, j * 128:(j + 1) * 128]), outs=["wo"], dma=dG)
            it = 0
            for qt4 in range(4):
                s.add("sync", lambda e, qt4=qt4: e.dma_start(
                    xt[:].rearrange("p (c t) -> p c t", t=QT),
                    xTown.rearrange("(c p) t -> p c t", p=128)
                    [:, :, qt4 * QT:(qt4 + 1) * QT]), outs=["a3xt"], dma=dS)
                for j in range(16):
                    pi = it % 2

                    def t_mm(e, j=j, qt4=qt4, pi=pi):
                        ins = None
                        for h in range(NH):
                            ins = e.matmul(
                                pss[pi][:],
                                wo_sb[:, j * 2048 + h * 128:j * 2048 + h * 128 + 128],
                                attnout_sb[:, h * TOWN + qt4 * QT:
                                           h * TOWN + (qt4 + 1) * QT],
                                start=(h == 0), stop=(h == NH - 1))
                        return ins
                    s.add("tensor", t_mm, ins=["wo", "attnout"], outs=[("wops", pi)])

                    s.add("scalar", lambda e, j=j, pi=pi: e.activation(
                        obs[pi][:], pss[pi][:], AF.Identity, bias=bo_cols[:, j:j + 1]),
                        ins=[("wops", pi), "bo"], outs=[("wob", pi)])

                    s.add("vector", lambda e, j=j, pi=pi: e.tensor_tensor(
                        out=x2s[pi][:], in0=obs[pi][:],
                        in1=xt[:, j * QT:(j + 1) * QT], op=ALU.add),
                        ins=[("wob", pi), "a3xt"], outs=[("x2b", pi)])

                    s.add("sync", lambda e, j=j, qt4=qt4, pi=pi: e.dma_start(
                        x2_scr[:, j * TOWN + qt4 * QT:j * TOWN + qt4 * QT + QT],
                        x2s[pi][:]), ins=[("x2b", pi)], outs=["x2_scr"], dma=dS)
                    it += 1
            s.add("sync", lambda e: None, extra=[(dS, dS.n)])
            s.flush()

    def emit_ffn():
        with (
            nc.sbuf_tensor(_u("b_xln"), [128, NCH * FT], BF16) as xln2,
            nc.sbuf_tensor(_u("b_abc"), [128, 8 * FT], BF16) as abc,
            nc.sbuf_tensor(_u("b_wacc"), [128, 8 * FT], F32) as wacc,
            nc.sbuf_tensor(_u("b_smid"), [128, NCH * FT], BF16) as smid,
        ):
            for tt in range(NFT):
                with contextlib.ExitStack() as _st:
                    EC = _st.enter_context
                    x2t2 = [EC(nc.sbuf_tensor(_u("bl_x2t"),
                                              [128, NCH * QT], F32))] * 2
                    xbf = EC(nc.sbuf_tensor(_u("bl_xbf"), [128, NCH * QT],
                                            BF16))
                    rows = EC(nc.sbuf_tensor(_u("bl_rows"), [1, 5 * QT], F32))
                    ps_s = EC(nc.psum_tensor(_u("bl_ps_s"), [1, QT], F32))
                    ps_q = EC(nc.psum_tensor(_u("bl_ps_q"), [1, QT], F32))
                    bca = EC(nc.sbuf_tensor(_u("bl_bca"), [128, QT], F32))
                    bcb = EC(nc.sbuf_tensor(_u("bl_bcb"), [128, QT], BF16))
                    s = Sched()
                    for sub in range(2):
                        su = sub % 2
                        base = tt * FT + sub * QT
                        xt = x2t2[su]
                        s.add("sync" if su == 0 else "gpsimd",
                              lambda e, base=base, xt=xt: e.dma_start(
                                  xt[:].rearrange("p (c t) -> p c t", t=QT),
                                  x2_scr.rearrange("p (c t) -> p c t", c=NCH)
                                  [:, :, base:base + QT]),
                              ins=["x2_scr"], outs=[("bx2t", 0)],
                              dma=dS if su == 0 else dG)
                        emit_ln_tile(
                            s, src_wide=xt[:],
                            dst_wide3=xln2[:].rearrange("p (c t) -> p c t",
                                                        t=FT)
                            [:, :, sub * QT:(sub + 1) * QT],
                            dst_key="xln2" if sub == 1 else ("xln2s", sub),
                            bufs=(xbf, rows, ps_s[:1, :], ps_q[:1, :],
                                  bca, bcb),
                            src_keys=[("bx2t", 0)], W=QT, su=0)
                    s.flush()
                if tt == 0 and "xln2" in tap_outs:
                    s = Sched()
                    s.add("sync", lambda e: e.dma_start(tap_outs["xln2"], xln2[:]),
                          ins=["xln2"], dma=dS)
                    s.add("sync", lambda e: None, extra=[(dS, dS.n)])
                    s.flush()

                with (
                    nc.sbuf_tensor(_u("br_wr"), [128, NCH * 128], BF16) as wr_sb,
                    nc.sbuf_tensor(_u("br_e"), [128, 128], F32) as e_sb,
                    nc.sbuf_tensor(_u("br_rows"), [128, 12], F32) as rws,
                    nc.sbuf_tensor(_u("br_mbf"), [128, 128], BF16) as mbf,
                    nc.sbuf_tensor(_u("br_mT"), [128, 8 * 128], BF16) as mT,
                    nc.psum_tensor(_u("br_psr"), [128, 128], F32) as psr,
                    nc.psum_tensor(_u("br_psT"), [128, 128], BF16) as psT,
                    nc.psum_tensor(_u("br_psal"), [128, QT], F32) as psal,
                    nc.psum_tensor(_u("br_psbc"), [128, QT], F32) as psbc2,
                ):
                    s = Sched()
                    s.add("gpsimd", lambda e: e.dma_start(
                        wr_sb[:].rearrange("p (c j) -> p c j", j=128),
                        w_router.rearrange("(c p) j -> p c j", p=128)),
                        outs=["wr"], dma=dG)
                    for sub in range(8):
                        def t_r(e, sub=sub):
                            ins = None
                            for c in range(NCH):
                                ins = e.matmul(
                                    psr[:],
                                    xln2[:, c * FT + sub * 128:c * FT + sub * 128 + 128],
                                    wr_sb[:, c * 128:(c + 1) * 128],
                                    start=(c == 0), stop=(c == NCH - 1))
                            return ins
                        s.add("tensor", t_r, ins=["wr", "xln2"], outs=["psr"])

                        def v_max(e):
                            e.reduce_max(rws[:, 0:1], psr[:], axis=AX.X)
                            e.drain()
                            e.tensor_scalar(out=rws[:, 1:2], in0=rws[:, 0:1],
                                            scalar1=-1.0, scalar2=None, op0=ALU.mult)
                            return e.drain()
                        s.add("vector", v_max, ins=["psr"], outs=["nmax"])

                        s.add("scalar", lambda e: e.activation(
                            e_sb[:], psr[:], AF.Exp, bias=rws[:, 1:2],
                            accum_out=rws[:, 2:3]),
                            ins=["psr", "nmax"], outs=["esb", "zrow"])

                        def v_top4(e):
                            e.max(out=rws[:, 4:12], in_=e_sb[:])
                            e.drain()
                            e.tensor_scalar(out=mbf[:], in0=e_sb[:],
                                            scalar1=rws[:, 7:8], scalar2=None,
                                            op0=ALU.is_ge)
                            e.tensor_tensor(out=e_sb[:], in0=e_sb[:], in1=mbf[:],
                                            op=ALU.mult)
                            e.drain()
                            e.reduce_sum(rws[:, 3:4], e_sb[:], axis=AX.X)
                            e.tensor_scalar(out=rws[:, 4:5], in0=rws[:, 2:3],
                                            scalar1=float(EPS_R), scalar2=None,
                                            op0=ALU.mult)
                            e.drain()
                            e.tensor_tensor(out=rws[:, 3:4], in0=rws[:, 3:4],
                                            in1=rws[:, 4:5], op=ALU.add)
                            e.drain()
                            e.reciprocal(rws[:, 3:4], rws[:, 3:4])
                            e.drain()
                            e.tensor_scalar(out=e_sb[:], in0=e_sb[:],
                                            scalar1=rws[:, 3:4], scalar2=None,
                                            op0=ALU.mult)
                            return e.tensor_tensor(out=mbf[:], in0=e_sb[:],
                                                   in1=esc_bc[:], op=ALU.mult)
                        s.add("vector", v_top4, ins=["esb", "zrow", "escbc"],
                              outs=["mbf"])

                        s.add("tensor", lambda e: e.transpose(psT[:], mbf[:],
                                                              ident_bf[:]),
                              ins=["mbf", "ident"], outs=["psT"])
                        s.add("vector", lambda e, sub=sub: e.tensor_copy(
                            mT[:, sub * 128:(sub + 1) * 128], psT[:]),
                            ins=["psT"], outs=["mT"])

                    alps = (psal, psbc2)
                    for kh in range(16):
                        k, half = kh // 2, kh % 2
                        pa = kh % 2
                        s.add("tensor", lambda e, k=k, half=half, pa=pa: e.matmul(
                            alps[pa][:], mixM[:, k * 128:(k + 1) * 128],
                            mT[:, half * QT:(half + 1) * QT],
                            start=True, stop=True),
                            ins=["mT", "mix"], outs=[("psal", pa)])
                        s.add("vector", lambda e, k=k, half=half, pa=pa:
                              e.tensor_copy(
                                  abc[:, k * FT + half * QT:
                                      k * FT + half * QT + QT],
                                  alps[pa][:]), ins=[("psal", pa)], outs=["abc"])
                    if tt == 0 and "mT" in tap_outs:
                        s.add("sync", lambda e: e.dma_start(tap_outs["mT"], mT[:]),
                              ins=["mT"], dma=dS)
                        s.add("sync", lambda e: e.dma_start(tap_outs["abc"], abc[:]),
                              ins=["abc"], dma=dS)
                        s.add("sync", lambda e: None, extra=[(dS, dS.n)])
                    s.flush()

                with (
                    nc.sbuf_tensor(_u("b2_wg0"), [128, NCH * 128], BF16) as wg0,
                    nc.sbuf_tensor(_u("b2_wg1"), [128, NCH * 128], BF16) as wg1,
                    nc.sbuf_tensor(_u("b2_wu0"), [128, NCH * 128], BF16) as wu0,
                    nc.sbuf_tensor(_u("b2_wu1"), [128, NCH * 128], BF16) as wu1,
                    nc.sbuf_tensor(_u("b2_sil0"), [128, FT], F32) as sil0,
                    nc.sbuf_tensor(_u("b2_sil1"), [128, FT], F32) as sil1,
                    nc.sbuf_tensor(_u("b2_bas0"), [128, FT], BF16) as bas0,
                    nc.sbuf_tensor(_u("b2_bas1"), [128, FT], BF16) as bas1,
                    nc.sbuf_tensor(_u("b2_bs0"), [128, FT], BF16) as bs0,
                    nc.sbuf_tensor(_u("b2_bs1"), [128, FT], BF16) as bs1,
                    nc.psum_tensor(_u("b2_psA"), [128, FT], F32) as psA,
                    nc.psum_tensor(_u("b2_psB"), [128, FT], F32) as psB,
                    nc.psum_tensor(_u("b2_psC"), [128, FT], F32) as psC,
                ):
                    wgs = (wg0, wg1); wus = (wu0, wu1); sils = (sil0, sil1)
                    bas = (bas0, bas1); bss = (bs0, bs1); gps = (psA, psB)
                    s = Sched()
                    for kic in range(64):
                        k, ic = kic // 8, kic % 8
                        pi = kic % 2
                        s.add("gpsimd", lambda e, kic=kic, pi=pi: e.dma_start(
                            wgs[pi][:].rearrange("p (c j) -> p c j", j=128),
                            w_gate.rearrange("(c p) j -> p c j", p=128)
                            [:, :, kic * 128:(kic + 1) * 128]),
                            outs=[("wg", pi)], dma=dG)
                        s.add("gpsimd", lambda e, kic=kic, pi=pi: e.dma_start(
                            wus[pi][:].rearrange("p (c j) -> p c j", j=128),
                            w_up.rearrange("(c p) j -> p c j", p=128)
                            [:, :, kic * 128:(kic + 1) * 128]),
                            outs=[("wu", pi)], dma=dG)

                        def t_g(e, pi=pi):
                            ins = None
                            for c in range(NCH):
                                ins = _mm2(
                                    e, gps[pi][:], wgs[pi][:, c * 128:(c + 1) * 128],
                                    xln2[:, c * FT:(c + 1) * FT],
                                    start=(c == 0), stop=(c == NCH - 1))
                            return ins
                        s.add("tensor", t_g, ins=[("wg", pi), "xln2"],
                              outs=[("gps", pi)])

                        def t_u(e, pi=pi):
                            ins = None
                            for c in range(NCH):
                                ins = _mm2(
                                    e, psC[:], wus[pi][:, c * 128:(c + 1) * 128],
                                    xln2[:, c * FT:(c + 1) * FT],
                                    start=(c == 0), stop=(c == NCH - 1))
                            return ins
                        s.add("tensor", t_u, ins=[("wu", pi), "xln2"], outs=["psC"])

                        s.add("scalar", lambda e, pi=pi: e.activation(
                            sils[pi][:], gps[pi][:], AF.Silu),
                            ins=[("gps", pi)], outs=[("sil", pi)])

                        s.add("vector", lambda e, pi=pi: e.tensor_tensor(
                            out=bas[pi][:], in0=sils[pi][:], in1=psC[:], op=ALU.mult),
                            ins=[("sil", pi), "psC"], outs=[("bas", pi)])

                        s.add("vector", lambda e, k=k, pi=pi: e.tensor_tensor(
                            out=bss[pi][:], in0=bas[pi][:],
                            in1=abc[:, k * FT:(k + 1) * FT], op=ALU.mult),
                            ins=[("bas", pi), "abc"], outs=[("bs", pi)])

                        if k == 0:
                            s.add("vector", lambda e, ic=ic, pi=pi: e.tensor_copy(
                                wacc[:, ic * FT:(ic + 1) * FT], bss[pi][:]),
                                ins=[("bs", pi)], outs=[("wacc", ic)])
                        else:
                            def v_acc(e, ic=ic, pi=pi):
                                return e.tensor_tensor(
                                    out=wacc[:, ic * FT:(ic + 1) * FT],
                                    in0=wacc[:, ic * FT:(ic + 1) * FT],
                                    in1=bss[pi][:], op=ALU.add)
                            s.add("vector", v_acc, ins=[("bs", pi), ("wacc", ic)],
                                  outs=[("wacc", ic)])
                    if tt == 0 and "wacc" in tap_outs:
                        s.add("sync", lambda e: e.dma_start(tap_outs["wacc"], wacc[:]),
                              ins=[("wacc", ic) for ic in range(8)], dma=dS)
                        s.add("sync", lambda e: None, extra=[(dS, dS.n)])
                    s.flush()

                with (
                    nc.sbuf_tensor(_u("b3_wg0"), [128, NCH * 128], BF16) as swg0,
                    nc.sbuf_tensor(_u("b3_wg1"), [128, NCH * 128], BF16) as swg1,
                    nc.sbuf_tensor(_u("b3_wu0"), [128, NCH * 128], BF16) as swu0,
                    nc.sbuf_tensor(_u("b3_wu1"), [128, NCH * 128], BF16) as swu1,
                    nc.sbuf_tensor(_u("b3_sil0"), [128, FT], F32) as ssil0,
                    nc.sbuf_tensor(_u("b3_sil1"), [128, FT], F32) as ssil1,
                    nc.psum_tensor(_u("b3_psA"), [128, FT], F32) as psA3,
                    nc.psum_tensor(_u("b3_psB"), [128, FT], F32) as psB3,
                    nc.psum_tensor(_u("b3_psC"), [128, FT], F32) as psC3,
                ):
                    wgs = (swg0, swg1); wus = (swu0, swu1); sils = (ssil0, ssil1)
                    gps = (psA3, psB3)
                    s = Sched()
                    for oc in range(16):
                        pi = oc % 2
                        s.add("gpsimd", lambda e, oc=oc, pi=pi: e.dma_start(
                            wgs[pi][:].rearrange("p (c j) -> p c j", j=128),
                            swg.rearrange("(c p) j -> p c j", p=128)
                            [:, :, oc * 128:(oc + 1) * 128]),
                            outs=[("swg", pi)], dma=dG)
                        s.add("gpsimd", lambda e, oc=oc, pi=pi: e.dma_start(
                            wus[pi][:].rearrange("p (c j) -> p c j", j=128),
                            swu.rearrange("(c p) j -> p c j", p=128)
                            [:, :, oc * 128:(oc + 1) * 128]),
                            outs=[("swu", pi)], dma=dG)

                        def t_g(e, pi=pi):
                            ins = None
                            for c in range(NCH):
                                ins = _mm2(
                                    e, gps[pi][:], wgs[pi][:, c * 128:(c + 1) * 128],
                                    xln2[:, c * FT:(c + 1) * FT],
                                    start=(c == 0), stop=(c == NCH - 1))
                            return ins
                        s.add("tensor", t_g, ins=[("swg", pi), "xln2"],
                              outs=[("sgps", pi)])

                        def t_u(e, pi=pi):
                            ins = None
                            for c in range(NCH):
                                ins = _mm2(
                                    e, psC3[:], wus[pi][:, c * 128:(c + 1) * 128],
                                    xln2[:, c * FT:(c + 1) * FT],
                                    start=(c == 0), stop=(c == NCH - 1))
                            return ins
                        s.add("tensor", t_u, ins=[("swu", pi), "xln2"], outs=["psC3"])

                        s.add("scalar", lambda e, pi=pi: e.activation(
                            sils[pi][:], gps[pi][:], AF.Silu),
                            ins=[("sgps", pi)], outs=[("ssil", pi)])

                        s.add("vector", lambda e, oc=oc, pi=pi: e.tensor_tensor(
                            out=smid[:, oc * FT:(oc + 1) * FT], in0=sils[pi][:],
                            in1=psC3[:], op=ALU.mult),
                            ins=[("ssil", pi), "psC3"], outs=["smid"])
                    if tt == 0 and "smid" in tap_outs:
                        s.add("sync", lambda e: e.dma_start(tap_outs["smid"], smid[:]),
                              ins=["smid"], dma=dS)
                        s.add("sync", lambda e: None, extra=[(dS, dS.n)])
                    s.flush()

                with (
                    nc.sbuf_tensor(_u("b4_wbf"), [128, 8 * FT], BF16) as wbf,
                    nc.sbuf_tensor(_u("b4_wd0"), [128, 8 * 128], BF16) as wd0,
                    nc.sbuf_tensor(_u("b4_wd1"), [128, 8 * 128], BF16) as wd1,
                    nc.sbuf_tensor(_u("b4_wsd0"), [128, NCH * 128], BF16) as wsd0,
                    nc.sbuf_tensor(_u("b4_wsd1"), [128, NCH * 128], BF16) as wsd1,
                    nc.sbuf_tensor(_u("b4_x20"), [128, FT], F32) as x20,
                    nc.sbuf_tensor(_u("b4_x21"), [128, FT], F32) as x21,
                    nc.sbuf_tensor(_u("b4_yb0"), [128, FT], F32) as yb0,
                    nc.sbuf_tensor(_u("b4_yb1"), [128, FT], F32) as yb1,
                    nc.psum_tensor(_u("b4_ps0"), [128, FT], F32) as bps0,
                    nc.psum_tensor(_u("b4_ps1"), [128, FT], F32) as bps1,
                ):
                    wds = (wd0, wd1); wsds = (wsd0, wsd1)
                    x2s = (x20, x21); ybs = (yb0, yb1); pss = (bps0, bps1)
                    s = Sched()

                    def v_cast(e):
                        return e.tensor_copy(wbf[:], wacc[:])
                    s.add("vector", v_cast,
                          ins=[("wacc", ic) for ic in range(8)], outs=["wbf"])

                    for j in range(16):
                        pi = j % 2
                        s.add("gpsimd", lambda e, j=j, pi=pi: e.dma_start(
                            wds[pi][:].rearrange("p (c jj) -> p c jj", jj=128),
                            w_down.rearrange("(c p) jj -> p c jj", p=128)
                            [:, :, j * 128:(j + 1) * 128]),
                            outs=[("wd", pi)], dma=dG)
                        s.add("gpsimd", lambda e, j=j, pi=pi: e.dma_start(
                            wsds[pi][:].rearrange("p (c jj) -> p c jj", jj=128),
                            swd.rearrange("(c p) jj -> p c jj", p=128)
                            [:, :, j * 128:(j + 1) * 128]),
                            outs=[("wsd", pi)], dma=dG)
                        s.add("sync", lambda e, j=j, tt=tt, pi=pi: e.dma_start(
                            x2s[pi][:],
                            x2_scr[:, j * TOWN + tt * FT:j * TOWN + tt * FT + FT]),
                            ins=["x2_scr"], outs=[("bx2", pi)], dma=dS)

                        def t_mm(e, j=j, pi=pi):
                            ins = None
                            for ic in range(8):
                                _mm2(e, pss[pi][:],
                                     wds[pi][:, ic * 128:(ic + 1) * 128],
                                     wbf[:, ic * FT:(ic + 1) * FT],
                                     start=(ic == 0), stop=False)
                            for oc in range(NCH):
                                ins = _mm2(e, pss[pi][:],
                                           wsds[pi][:, oc * 128:(oc + 1) * 128],
                                           smid[:, oc * FT:(oc + 1) * FT],
                                           start=False, stop=(oc == NCH - 1))
                            return ins
                        s.add("tensor", t_mm,
                              ins=[("wd", pi), ("wsd", pi), "wbf", "smid"],
                              outs=[("bps", pi)])

                        s.add("vector", lambda e, pi=pi: e.tensor_tensor(
                            out=ybs[pi][:], in0=pss[pi][:], in1=x2s[pi][:],
                            op=ALU.add),
                            ins=[("bps", pi), ("bx2", pi)], outs=[("yb", pi)])

                        s.add("sync", lambda e, j=j, tt=tt, pi=pi: e.dma_start(
                            yT[j * 128:(j + 1) * 128, tt * FT:(tt + 1) * FT],
                            ybs[pi][:]), ins=[("yb", pi)], outs=["yT"], dma=dS)
                    s.add("sync", lambda e: None, extra=[(dS, dS.n)])
                    s.flush()

    def tap(name, src):
        if name in tap_outs:
            s = Sched()
            s.add("sync", lambda e: e.dma_start(tap_outs[name], src), dma=dS)
            s.add("sync", lambda e: None, extra=[(dS, dS.n)])
            s.flush()

    if "A0b" in phases or "A1q" in phases:
        emit_phase_A0bq()
    if "A0" in phases or "A1k" in phases or "A1v" in phases:
        with nc.sbuf_tensor(_u("xlnP"), [128, NCH * SEQ], BF16) as xlnP:
            emit_phase_A0k(xlnP)
            if "A1v" in phases:
                emit_proj_v(xlnP, [("xlnP", j) for j in range(NLT)])
    tap("q_scr", q_scr)
    tap("k_scr", k_scr)
    tap("v_scr", v_scr)
    if "A2" in phases or "A3" in phases:
        with nc.sbuf_tensor(_u("attnout"), [128, NH * TOWN], BF16) as attnout_sb:
            if "A2" in phases:
                emit_attn(attnout_sb)
                tap("attnout", attnout_sb[:])
            if "A3" in phases:
                emit_wo(attnout_sb)
    tap("x2", x2_scr)
    if "B" in phases:
        emit_ffn()

    s = Sched()
    s.add("sync", lambda e: None, extra=[(dS, dS.n), (dG, dG.n)])
    s.add("gpsimd", lambda e: None, extra=[(dG, dG.n)])
    s.flush()
    return nc


def _scram_cols(par):
    own = OWN_BLOCKS[par]
    jj = np.arange(QT)
    off = (jj % 32) * 16 + jj // 32
    return np.concatenate([blk * QT + off for blk in own])


def _qsrc_cols(par):
    own = OWN_BLOCKS[par]
    cols = []
    for H in range(16):
        for blk in own:
            cols.append(H * 256 + blk * 32 + np.arange(32))
    return np.concatenate(cols)


def _core_inputs(inputs, core):
    b = core // 2
    par = core % 2
    x = inputs["L0_attn_x"]
    xT_b = np.ascontiguousarray(x[b].T)
    own = OWN_BLOCKS[par]
    cols = _scram_cols(par)
    xTown = np.ascontiguousarray(xT_b[:, cols])
    xQown_flat = xT_b[:, _qsrc_cols(par)]
    xT_tiled = np.ascontiguousarray(
        x[b].reshape(16, 256, 16, 128).transpose(0, 3, 2, 1)
        .reshape(16 * 128, 16 * 256))
    xQown_tiled = np.ascontiguousarray(
        xQown_flat.T.reshape(8, 256, 16, 128).transpose(0, 3, 2, 1)
        .reshape(8 * 128, 16 * 256))
    nd = np.zeros((1, 32), np.float32)
    for qi in range(NQT):
        B = own[qi] * QT
        for r in range(8):
            kb = KBMAX[qi] - 8 + r
            nd[0, qi * 8 + r] = 128 * kb - B
    return {
        "xT": xT_tiled, "xTown": xTown, "xQown": xQown_tiled, "negdelta": nd,
        "Wq": inputs["L0_attn_Wq"], "Wk": inputs["L0_attn_Wk"],
        "Wv": inputs["L0_attn_Wv"], "Wo": inputs["L0_attn_Wo"],
        "bq": inputs["L0_attn_bq"], "bk": inputs["L0_attn_bk"],
        "bv": inputs["L0_attn_bv"], "bo": inputs["L0_attn_bo"],
        "w_router": inputs["L0_ffn_w_router"],
        "w_gate": inputs["L0_ffn_w_gate"], "w_up": inputs["L0_ffn_w_up"],
        "w_down": inputs["L0_ffn_w_down"],
        "amp2": np.ascontiguousarray(
            inputs["L0_ffn_amplitudes"].reshape(128, 16)),
        "esc": np.ascontiguousarray(
            inputs["L0_ffn_expert_scale"].reshape(1, 128)),
        "swg": inputs["L0_ffn_shared_w_gate"],
        "swu": inputs["L0_ffn_shared_w_up"],
        "swd": inputs["L0_ffn_shared_w_down"],
    }


def kernel(**inputs):
    nc = build_program()
    in_maps = [
        {k: np.ascontiguousarray(np.asarray(v, dtype=np.float32))
         for k, v in _core_inputs(inputs, c).items()}
        for c in range(8)
    ]
    res = run_bass_kernel_spmd(nc, in_maps, core_ids=list(range(8)))
    out = np.empty((4, SEQ, D), np.float32)
    for c in range(8):
        b = c // 2
        cols = _scram_cols(c % 2)
        yT_c = res.results[c]["yT"]
        out[b, cols, :] = yT_c.T
    return out
